# version 49
# baseline (speedup 1.0000x reference)
import numpy as np
import ml_dtypes
from contextlib import ExitStack

import concourse.bass as bass
import concourse.mybir as mybir
from concourse.bass_utils import run_bass_kernel_spmd

F32 = mybir.dt.float32
BF16 = mybir.dt.bfloat16
AF = mybir.ActivationFunctionType
ALU = mybir.AluOpType

D = 1024
FF = 4096
NH = 16
DH = 64
SEQ = 2048
GT = 512
NG = 8
NCHUNK = 4
EPS_LN = 1e-5
EPS_ATT = 1e-6


class Prog:
    ENGS = ("tensor", "vector", "scalar", "sync", "gpsimd")

    def __init__(self):
        self.ops = {e: [] for e in self.ENGS}
        self.count = {}
        self.waited = {e: {} for e in self.ENGS}

    def tok(self, sem):
        return (sem, self.count.get(sem, 0))

    def emit(self, eng, fn, waits=(), inc=None):
        ws = []
        for w in waits:
            if w is None:
                continue
            s, v = w
            if v > 0 and self.waited[eng].get(s, 0) < v:
                self.waited[eng][s] = v
                ws.append((s, v))
        tok = None
        if inc is not None:
            s, k = inc
            self.count[s] = self.count.get(s, 0) + k
            tok = (s, self.count[s])
        self.ops[eng].append((ws, fn, inc))
        return tok

    def run(self, eng_name, handle, sems):
        for ws, fn, inc in self.ops[eng_name]:
            for s, v in ws:
                handle.wait_ge(sems[s], v)
            inst = fn(handle)
            if inc is not None:
                s, k = inc
                inst.then_inc(sems[s], k)


def build_nc(n_groups=NG):
    nc = bass.Bass("TRN2", target_bir_lowering=False, detect_race_conditions=False)

    ap = lambda name, shape, dt: nc.dram_tensor(name, shape, dt, kind="ExternalInput").ap()
    xT = ap("xT", [2, D, SEQ], F32)
    wqT = ap("wqT", [D, D], F32)
    wkT = ap("wkT", [D, D], F32)
    wv_d = ap("wv", [D, D], F32)
    wo_d = ap("wo", [D, D], F32)
    pblk = ap("pblk", [128, 128], F32)
    wrT = ap("wrT", [D, D], F32)
    wkT2 = ap("wkT2", [D, D], F32)
    wvT2 = ap("wvT2", [D, D], F32)
    w1_d = ap("w1", [D, FF], F32)
    w2_d = ap("w2", [FF, D], F32)
    vecs = ap("vecs", [128, 64], F32)
    b1c_d = ap("b1c", [128, 32], F32)
    maskc = ap("maskc", [128, 512], F32)
    idc = ap("idc", [128, 128], F32)
    oT = nc.dram_tensor("oT", [2, D, SEQ], F32, kind="ExternalOutput").ap()
    wsc_r = nc.dram_tensor("wsc_r", [24, 128, 1024], BF16, kind="Internal").ap()
    wsc_1 = nc.dram_tensor("wsc_1", [32, 128, 1024], BF16, kind="Internal").ap()
    wsc_2 = nc.dram_tensor("wsc_2", [16, 128, 2048], BF16, kind="Internal").ap()

    P = Prog()
    E = P.emit

    with ExitStack() as es:
        sb = lambda name, shape, dt: es.enter_context(nc.sbuf_tensor(name, shape, dt))
        wqp = sb("wqp", [128, 8 * D], BF16)
        wkp = sb("wkp", [128, 8 * D], BF16)
        wv = sb("wv_sb", [128, 8 * D], BF16)
        xbuf = [sb("xa", [128, 8 * GT], F32), sb("xb2", [128, 8 * GT], F32)]
        xn = sb("xn", [128, 8 * GT], F32)
        cast = sb("cast", [128, 8 * GT], BF16)
        hT = sb("hT", [128, 8 * GT], BF16)
        u = sb("u", [128, 16448], BF16)
        tmp = sb("tmp", [128, GT + 1], F32)
        am = sb("am", [128, 512], BF16)
        wo = sb("wo_sb", [128, 8 * D], BF16)
        den = sb("den", [128, 2], F32)
        rcp = sb("rcp", [128, 2], F32)
        kv = sb("kv", [128, NH * 65], F32)
        kvb = sb("kvb", [128, NH * 65], BF16)
        carry = sb("carry", [128, 8], F32)
        rows = sb("rows", [128, 3 * GT + 128], F32)
        vcol = sb("vcol", [128, 64], F32)
        b1c = sb("b1c_sb", [128, 32], F32)
        maskb = sb("maskb", [128, 512], BF16)
        idb = sb("idb", [128, 128], BF16)
        onesb = sb("onesb", [128, 1], BF16)
        epsc = sb("epsc", [128, 2], F32)
        wr_ring = sb("wr_ring", [128, 4 * 1024], BF16)
        r_ring = sb("r_ring", [128, 2 * 512], BF16)
        w1_ring = sb("w1_ring", [128, 4 * 1024], BF16)
        w2_ring = sb("w2_ring", [128, 3 * 2048], BF16)

        ps = [es.enter_context(nc.psum_tensor(f"ps{i}", [128, 512], F32)) for i in range(8)]
        sems = {}
        for s in (["s_x", "s_o", "s_w", "s_t", "s_v", "s_s"]
                  + [f"s_wr{i}" for i in range(4)]
                  + [f"s_w1{i}" for i in range(4)]
                  + [f"s_w2{i}" for i in range(3)] + ["s_o0", "s_o1"]
                  + [f"s_wr{i}h" for i in range(4)]
                  + [f"s_w1{i}h" for i in range(4)]
                  + [f"s_w2{i}h" for i in range(3)]
                  + ["s_sc", "s_g"]):
            sems[s] = es.enter_context(nc.semaphore(s))
        block = es.enter_context(nc.Block())

        e1 = am
        q_r = u[:, 0:4096]
        k_r = u[:, 4096:8192]
        kt_r = u[:, 8192:12288]
        va_r = u[:, 12288:16448]
        y1_r = u[:, 0:16384]
        xr_r = kt_r
        xk_r = u[:, 12288:16384]
        rM = rows[0:1, 0:GT]
        rV = rows[0:1, GT:2 * GT]
        rR = rows[0:1, 2 * GT:3 * GT]
        ones128 = rows[0:1, 3 * GT:3 * GT + 128]

        g1c = vcol[:, 0:8]
        be1c = vcol[:, 8:16]
        g2c = vcol[:, 16:24]
        be2c = vcol[:, 24:32]
        murc = vcol[:, 32:40]
        mukc = vcol[:, 40:48]
        b2c = vcol[:, 48:56]

        def _ld(dst, src):
            return lambda h, d=dst, s=src: h.dma_start(out=d, in_=s)

        E("sync", _ld(vcol[:, :], vecs), inc=("s_x", 16))
        E("sync", _ld(b1c[:, :], b1c_d), inc=("s_x", 16))
        t_vec = P.tok("s_x")
        E("sync", _ld(xbuf[0][:].rearrange("p (k t) -> p k t", t=GT),
                      xT.rearrange("b (k p) t -> b p k t", p=128)[0:1, :, :, 0:GT]
                        .rearrange("b p k t -> p (b k) t")),
          inc=("s_x", 16))
        t_x0 = P.tok("s_x")

        E("gpsimd", _ld(u[:, 0:8192].rearrange("p (k c) -> p k c", c=1024),
                        wqT.rearrange("(k p) c -> p k c", p=128)), inc=("s_w", 16))
        E("gpsimd", _ld(u[:, 8192:16384].rearrange("p (k c) -> p k c", c=1024),
                        wkT.rearrange("(k p) c -> p k c", p=128)), inc=("s_w", 16))
        E("gpsimd", _ld(am[:, 0:128], pblk), inc=("s_w", 16))
        t_foldw = P.tok("s_w")
        E("gpsimd", _ld(wv[:].rearrange("p (k c) -> p k c", c=1024),
                        wv_d.rearrange("(k p) c -> p k c", p=128)), inc=("s_w", 16))
        t_wv = P.tok("s_w")
        E("gpsimd", _ld(maskb[:, :], maskc), inc=("s_w", 16))
        t_maskb = P.tok("s_w")
        E("gpsimd", _ld(idb[:, :], idc), inc=("s_w", 16))
        t_idb = P.tok("s_w")
        E("gpsimd", _ld(wo[:].rearrange("p (k c) -> p k c", c=1024),
                        wo_d.rearrange("(k p) c -> p k c", p=128)), inc=("s_w", 16))
        t_wstart = P.tok("s_w")

        E("vector", lambda h: h.memset(onesb[:, :], 1.0), inc=("s_v", 1))
        E("vector", lambda h: h.memset(epsc[:, 0:1], EPS_LN), inc=("s_v", 1))
        E("vector", lambda h: h.memset(epsc[:, 1:2], EPS_ATT), inc=("s_v", 1))
        E("vector", lambda h: h.memset(ones128, 1.0), inc=("s_v", 1))
        E("vector", lambda h: h.memset(kv[:, :], 0.0), inc=("s_v", 1))
        E("vector", lambda h: h.memset(kvb[:, :], 0.0), inc=("s_v", 1))
        E("vector", lambda h: h.memset(carry[:, :], 0.0), inc=("s_v", 1))
        t_const = P.tok("s_v")

        fb_free = {2: None, 3: None}
        for dst, stof in ((wqp, 0), (wkp, 8192)):
            for m in range(8):
                for jg in range(2):
                    bank = 2 + jg
                    for j4 in range(4):
                        j = jg * 4 + j4
                        def mm_fold(h, b=bank, j4=j4, j=j, m=m, so=stof):
                            return h.matmul(ps[b][:, j4 * 128:(j4 + 1) * 128],
                                            u[:, so + j * 1024 + m * 128: so + j * 1024 + (m) * 128 + 128],
                                            am[:, 0:128], start=True, stop=True)
                        tk = E("tensor", mm_fold,
                               waits=[t_foldw, t_const, fb_free[bank]],
                               inc=("s_t", 1) if j4 == 3 else None)
                    def ev_fold(h, b=bank, d=dst, m=m, jg=jg):
                        return h.tensor_copy(d[:, m * 1024 + jg * 512: m * 1024 + (jg + 1) * 512],
                                             ps[b][:, 0:512])
                    fb_free[bank] = E("vector", ev_fold, waits=[tk], inc=("s_v", 1))
        t_fold = P.tok("s_v")

        sc_written = {}
        slot_sc = {}

        def stream(dst_ap, src_ap, sem, waits=(), scratch=None, g=0, flat=None, q2=None):
            if scratch is None or g == 0:
                def fn(h, d=dst_ap, s=src_ap):
                    return h.dma_start(out=d, in_=s)
                E("gpsimd", fn, waits=list(waits) + [slot_sc.get(sem)], inc=(sem, 16))
                tok = P.tok(sem)
                if scratch is not None:
                    kind, blk, sc_ap = scratch
                    def wr(h, d=sc_ap, s=flat):
                        return h.dma_start(out=d, in_=s)
                    E("scalar", wr, waits=[tok], inc=("s_sc", 16))
                    sc_written[(kind, blk)] = P.tok("s_sc")
                    slot_sc[sem] = sc_written[(kind, blk)]
                return tok
            else:
                kind, blk, sc_ap = scratch
                q = q2 or ("gpsimd" if kind == "r" else "sync")
                def fn(h, d=flat, s=sc_ap):
                    return h.dma_start(out=d, in_=s)
                E(q, fn, waits=list(waits) + [sc_written[(kind, blk)]], inc=(sem + "h", 16))
                return P.tok(sem + "h")

        bank_free = {i: None for i in range(8)}
        bank_free[0] = t_const
        bank_free[1] = t_const
        bank_free[2] = fb_free[2]
        bank_free[3] = fb_free[3]
        big_free = bank_free
        _bigc = [0]

        def nbig():
            b = (0, 1, 6, 7)[_bigc[0] % 4]
            _bigc[0] += 1
            return b

        _bigc6 = [0]

        def nbig6():
            b = (0, 1, 6, 7, 2, 3)[_bigc6[0] % 6]
            _bigc6[0] += 1
            return b

        out_toks = []
        kv_ready = t_const
        wr_last = [t_fold, t_fold]
        w1_last = [t_fold] * 4
        w2_last = [t_fold] * 3

        wr_last = [t_fold] * 4

        def emit_xload(g):
            batch, c4 = g // 4, g % 4
            gslice = slice(c4 * GT, (c4 + 1) * GT)
            xg = xbuf[g % 2]
            w = [out_toks[g - 2]] if g >= 2 else []
            return E("scalar", _ld(xg[:].rearrange("p (k t) -> p k t", t=GT),
                                 xT.rearrange("b (k p) t -> b p k t", p=128)[batch:batch + 1, :, :, gslice]
                                   .rearrange("b p k t -> p (b k) t")),
                     waits=w, inc=("s_x", 16))

        def emit_ln1(g, t_x, t_mlp1_prev, mlp2_hook=None):
            xg = xbuf[g % 2]
            xb_tok = {}
            for kt in range(8):
                sl = slice(kt * GT, (kt + 1) * GT)
                xb_tok[kt] = E("scalar", lambda h, x=xg, sl=sl: h.activation(
                    cast[:, sl], x[:, sl], AF.Identity),
                    waits=[t_x, t_mlp1_prev], inc=("s_s", 1))
            t_xb = P.tok("s_s")
            if mlp2_hook: mlp2_hook(2)
            for kt in range(8):
                def mm_s1(h, kt=kt):
                    return h.matmul(ps[4][0:1, 0:512], onesb[:, 0:1],
                                    cast[:, kt * GT:(kt + 1) * GT],
                                    start=(kt == 0), stop=(kt == 7))
                E("tensor", mm_s1, waits=[xb_tok[kt], bank_free[4], t_fold],
                  inc=("s_t", 1) if kt == 7 else None)
            t_stats1 = P.tok("s_t")
            t_sq = E("vector", lambda h: h.tensor_mul(cast[:, :], cast[:, :], cast[:, :]),
                     waits=[t_stats1], inc=("s_v", 1))
            if mlp2_hook: mlp2_hook(4)
            for kt in range(8):
                def mm_s2(h, kt=kt):
                    return h.matmul(ps[5][0:1, 0:512], onesb[:, 0:1],
                                    cast[:, kt * GT:(kt + 1) * GT],
                                    start=(kt == 0), stop=(kt == 7))
                E("tensor", mm_s2, waits=[t_sq, bank_free[5]],
                  inc=("s_t", 1) if kt == 7 else None)
            t_stats2 = P.tok("s_t")
            E("vector", lambda h: h.tensor_scalar_mul(rM, ps[4][0:1, 0:512], 1.0 / D),
              waits=[t_stats1], inc=("s_v", 1))
            E("vector", lambda h: h.tensor_mul(rV, rM, rM), inc=("s_v", 1))
            t_mrow = E("vector", lambda h: h.scalar_tensor_tensor(rV, ps[5][0:1, 0:512], 1.0 / D, rV,
                                                                 ALU.mult, ALU.subtract),
                       waits=[t_stats2], inc=("s_v", 1))
            bank_free[4] = t_mrow
            bank_free[5] = t_mrow
            t_sqr = E("scalar", lambda h: h.activation(rR, rV, AF.Sqrt, bias=epsc[0:1, 0:1]),
                      waits=[t_mrow], inc=("s_s", 1))
            t_rstd = E("vector", lambda h: h.reciprocal(rR, rR), waits=[t_sqr], inc=("s_v", 1))
            E("vector", lambda h: h.tensor_mul(rV, rV, rR), waits=[t_rstd], inc=("s_v", 1))
            t_v2 = E("vector", lambda h: h.tensor_mul(rV, rV, rR), inc=("s_v", 1))
            t_s2r = E("scalar", lambda h: h.activation(rV, rV, AF.Sqrt, bias=epsc[0:1, 0:1]),
                      waits=[t_v2], inc=("s_s", 1))
            t_r2 = E("vector", lambda h: h.reciprocal(rV, rV), waits=[t_s2r], inc=("s_v", 1))
            if mlp2_hook: mlp2_hook(6)
            t_mbc = E("tensor", lambda h: h.matmul(ps[2][:, 0:512], ones128, rM,
                                                   start=True, stop=True),
                      waits=[t_r2, bank_free[2]], inc=("s_t", 1))
            t_rbc = E("tensor", lambda h: h.matmul(ps[3][:, 0:512], ones128, rR,
                                                   start=True, stop=True),
                      waits=[bank_free[3]], inc=("s_t", 1))
            t_r2bc = E("tensor", lambda h: h.matmul(ps[4][:, 0:512], ones128, rV,
                                                    start=True, stop=True),
                       waits=[bank_free[4]], inc=("s_t", 1))
            for kt in range(8):
                sl = slice(kt * GT, (kt + 1) * GT)
                E("vector", lambda h, sl=sl, x=xg: h.tensor_sub(xn[:, sl], x[:, sl], ps[2][:, 0:512]),
                  waits=[t_mbc, t_x], inc=("s_v", 1))
            for kt in range(8):
                sl = slice(kt * GT, (kt + 1) * GT)
                E("vector", lambda h, sl=sl: h.tensor_mul(xn[:, sl], xn[:, sl], ps[3][:, 0:512]),
                  waits=[t_rbc], inc=("s_v", 1))
            t_xn = P.tok("s_v")
            if mlp2_hook: mlp2_hook(7)
            h_tok = {}
            for kt in range(8):
                sl = slice(kt * GT, (kt + 1) * GT)
                E("vector", lambda h, sl=sl: h.tensor_mul(hT[:, sl], xn[:, sl], ps[4][:, 0:512]),
                  waits=[t_r2bc, t_xn], inc=("s_v", 1))
                h_tok[kt] = E("vector", lambda h, sl=sl, kt=kt: h.tensor_scalar(
                    hT[:, sl], hT[:, sl], g1c[:, kt:kt + 1], be1c[:, kt:kt + 1], ALU.mult, ALU.add),
                  waits=[t_vec], inc=("s_v", 1))
            t_h = P.tok("s_v")
            bank_free[2] = t_xn
            bank_free[3] = t_xn
            bank_free[4] = t_h
            if mlp2_hook: mlp2_hook(8)
            return {"t_h": t_h, "t_xn": t_xn, "t_stats2": t_stats2, "h_tok": h_tok}

        ln1_toks = emit_ln1(0, t_x0, ("s_t", 0))
        out_toks = []
        kv_ready = t_const
        mlp2_done = ("s_t", 0)
        r_read_last = [("s_v", 0), ("s_v", 0)]
        phi_prev = [None, None]

        for g in range(n_groups):
            batch, c4 = g // 4, g % 4
            xg = xbuf[g % 2]
            gslice = slice(c4 * GT, (c4 + 1) * GT)
            t_h = ln1_toks["t_h"]
            h_tok = ln1_toks["h_tok"]
            t_xn = ln1_toks["t_xn"]
            t_stats2 = ln1_toks["t_stats2"]

            if g == 4:
                tA = P.tok("s_t")
                tS = P.tok("s_s")
                tV = P.tok("s_v")
                E("vector", lambda h: h.memset(kv[:, :], 0.0), waits=[tA, tS, tV], inc=("s_v", 1))
                E("vector", lambda h: h.memset(kvb[:, :], 0.0), inc=("s_v", 1))
                t_reset = P.tok("s_v")
                kv_ready = t_reset
            else:
                t_reset = None

            t_vaok = P.tok("s_t")
            t_vones = E("vector", lambda h: h.memset(
                va_r[:].rearrange("p (c hh o) -> p c hh o", hh=NH, o=65)[:, :, :, 64:65], 1.0),
                waits=[t_vaok, t_h], inc=("s_v", 1))
            e1bufs = (e1[:, 0:512], tmp[:, 0:512])

            def phi_evac(dst, msl, bk, wtok, idx):
                eb = e1bufs[idx % 2]
                w0 = [wtok, phi_prev[idx % 2]]
                if idx == 0:
                    w0 += [P.tok("s_t"), r_read_last[1]]
                t2 = E("scalar", lambda h, bk=bk, eb=eb: h.activation(eb, ps[bk][:, 0:512], AF.Exp),
                       waits=w0, inc=("s_s", 1))
                t1 = E("vector", lambda h, eb=eb: h.tensor_scalar_min(eb, eb, 1.0),
                       waits=[t2], inc=("s_v", 1))
                t3 = E("vector", lambda h, dst=dst, msl=msl, bk=bk, eb=eb: h.scalar_tensor_tensor(
                    dst[:, msl], ps[bk][:, 0:512], 0.0, eb, ALU.max, ALU.add),
                    waits=[t1], inc=("s_v", 1))
                phi_prev[idx % 2] = t3
                return t3

            for qi, (wmat, dstbuf) in enumerate(((wqp, q_r), (wkp, k_r))):
                for m in range(8):
                    bk = nbig6()
                    for kt in range(8):
                        def mm_q(h, bk=bk, kt=kt, m=m, w=wmat):
                            return h.matmul(ps[bk][:, 0:512],
                                            w[:, kt * 1024 + m * 128: kt * 1024 + (m + 1) * 128],
                                            hT[:, kt * GT:(kt + 1) * GT],
                                            start=(kt == 0), stop=(kt == 7))
                        E("tensor", mm_q, waits=[t_h, bank_free[bk], t_fold],
                          inc=("s_t", 1) if kt == 7 else None)
                    t_mm = P.tok("s_t")
                    bank_free[bk] = phi_evac(dstbuf, slice(m * GT, (m + 1) * GT), bk, t_mm,
                                             qi * 8 + m)
            t_qk = P.tok("s_v")


            evt_tok = {}

            def emit_V(tau, half, bk):
                for kt in range(8):
                    def mm_v(h, bk=bk, kt=kt, tau=tau, half=half):
                        return h.matmul(ps[bk][:, 0:512],
                                        hT[:, kt * GT + tau * 128: kt * GT + tau * 128 + 128],
                                        wv[:, kt * 1024 + half * 512: kt * 1024 + (half + 1) * 512],
                                        start=(kt == 0), stop=(kt == 7))
                    E("tensor", mm_v, waits=[bank_free[bk], t_vones, t_wv],
                      inc=("s_t", 1) if kt == 7 else None)
                t_mm = P.tok("s_t")
                def ev_v(h, bk=bk, tau=tau, half=half):
                    dst = va_r[:, tau * 1040 + half * 8 * 65: tau * 1040 + (half * 8 + 8) * 65]                         .rearrange("p (hh c) -> p hh c", c=65)[:, :, 0:64]
                    src = ps[bk][:, 0:512].rearrange("p (hh c) -> p hh c", c=64)
                    return h.tensor_copy(dst, src)
                bank_free[bk] = E("vector", ev_v, waits=[t_mm], inc=("s_v", 1))

            def emit_Kt(tau, eg, bk):
                for e4 in range(4):
                    et = eg * 4 + e4
                    def mm_t(h, bk=bk, e4=e4, et=et, tau=tau):
                        return h.matmul(ps[bk][:, e4 * 128:(e4 + 1) * 128],
                                        k_r[:, et * GT + tau * 128: et * GT + tau * 128 + 128],
                                        idb[:, :], start=True, stop=True)
                    E("tensor", mm_t, waits=[t_qk, bank_free[bk], t_idb],
                      inc=("s_t", 1) if e4 == 3 else None)
                t_mm = P.tok("s_t")
                def ev_t(h, bk=bk, tau=tau, eg=eg):
                    return h.activation(kt_r[:, tau * 1024 + eg * 512: tau * 1024 + (eg + 1) * 512],
                                        ps[bk][:, 0:512], AF.Identity)
                evt_tok[(tau, eg)] = E("scalar", ev_t, waits=[t_mm], inc=("s_s", 1))
                bank_free[bk] = evt_tok[(tau, eg)]

            for half in range(2):
                emit_V(0, half, nbig())
            for eg in range(2):
                emit_Kt(0, eg, nbig())

            def emit_A(c, hd):
                po = 64 * (hd % 2)
                et = hd // 2
                aslot = hd % 4
                abank = 2 + (hd % 2)
                ksl = slice(et * GT + c * 128, et * GT + c * 128 + 128)
                t_a = E("tensor", lambda h, po=po, ksl=ksl, ab=abank: h.matmul(
                    ps[ab][0:128, 0:128], k_r[po:po + 64, ksl], q_r[po:po + 64, ksl],
                    start=True, stop=True),
                    waits=[evt_tok[(c, hd // 8)], bank_free[abank]], inc=("s_t", 1))
                t_am = E("vector", lambda h, a=aslot, ab=abank: h.tensor_tensor(
                    am[:, a * 128:(a + 1) * 128], ps[ab][0:128, 0:128], maskb[:, 0:128], ALU.mult),
                    waits=[t_a, t_maskb], inc=("s_v", 1))
                bank_free[abank] = t_am
                return t_am

            if g == 4:
                E("vector", lambda h: h.memset(carry[:, :], 0.0), inc=("s_v", 1))

            mix_pend = []

            def queue_mix(kt):
                s0 = kt * GT
                tsb = P.tok("s_t")
                mix_pend.append(lambda tsb=tsb, s0=s0: E(
                    "vector", lambda h, s0=s0: h.tensor_sub(
                        tmp[:, 1:512], xn[:, s0 + 1:s0 + 512], xn[:, s0:s0 + 511]),
                    waits=[tsb, t_xn], inc=("s_v", 1)))
                mix_pend.append(lambda s0=s0, kt=kt: E(
                    "vector", lambda h: h.tensor_sub(
                        tmp[:, 0:1], xn[:, s0:s0 + 1], carry[:, kt:kt + 1]), inc=("s_v", 1)))
                mix_pend.append(lambda s0=s0, kt=kt: E(
                    "vector", lambda h: h.scalar_tensor_tensor(
                        xr_r[:, s0 + 1:s0 + 512], tmp[:, 1:512], murc[:, kt:kt + 1],
                        xn[:, s0:s0 + 511], ALU.mult, ALU.add), inc=("s_v", 1)))
                mix_pend.append(lambda s0=s0, kt=kt: E(
                    "vector", lambda h: h.scalar_tensor_tensor(
                        xr_r[:, s0:s0 + 1], tmp[:, 0:1], murc[:, kt:kt + 1],
                        carry[:, kt:kt + 1], ALU.mult, ALU.add), inc=("s_v", 1)))
                mix_pend.append(lambda s0=s0, kt=kt: E(
                    "vector", lambda h: h.scalar_tensor_tensor(
                        xk_r[:, s0 + 1:s0 + 512], tmp[:, 1:512], mukc[:, kt:kt + 1],
                        xn[:, s0:s0 + 511], ALU.mult, ALU.add), inc=("s_v", 1)))
                mix_pend.append(lambda s0=s0, kt=kt: E(
                    "vector", lambda h: h.scalar_tensor_tensor(
                        xk_r[:, s0:s0 + 1], tmp[:, 0:1], mukc[:, kt:kt + 1],
                        carry[:, kt:kt + 1], ALU.mult, ALU.add), inc=("s_v", 1)))

            def drain_mix(n):
                for _ in range(min(n, len(mix_pend))):
                    mix_pend.pop(0)()

            o_tok = {}
            side_pend = []

            def drain_side(n):
                for _ in range(min(n, len(side_pend))):
                    side_pend.pop(0)()
            for c in range(NCHUNK):
                if c + 1 < NCHUNK:
                    side_pend.append(lambda c1=c + 1: emit_V(c1, 0, 1))
                    side_pend.append(lambda c1=c + 1: emit_V(c1, 1, 1))
                    side_pend.append(lambda c1=c + 1: emit_Kt(c1, 0, 1))
                    side_pend.append(lambda c1=c + 1: emit_Kt(c1, 1, 1))
                am_toks = {}
                for hh in range(4):
                    am_toks[hh] = emit_A(c, hh)
                for hp in range(NH // 2):
                    nbank = (4, 5, 0)[hp % 3]
                    pair_n = {}
                    for sub in range(2):
                        hd = 2 * hp + sub
                        po = 64 * (hd % 2)
                        et = hd // 2
                        sbank = 6 + (hd % 2)
                        ksl = slice(et * GT + c * 128, et * GT + c * 128 + 128)
                        vsl = slice(c * 1040 + hd * 65, c * 1040 + (hd + 1) * 65)
                        aslot = hd % 4
                        co = sub * 65
                        E("tensor", lambda h, a=aslot, nb=nbank, vsl=vsl, co=co: h.matmul(
                            ps[nb][0:128, co:co + 65], am[:, a * 128:(a + 1) * 128], va_r[:, vsl],
                            start=True, stop=False),
                            waits=[am_toks[hd], bank_free[nbank] if sub == 0 else None, kv_ready])
                        t_n = E("tensor", lambda h, po=po, ksl=ksl, nb=nbank, hd=hd, co=co: h.matmul(
                            ps[nb][0:128, co:co + 65], q_r[po:po + 64, ksl],
                            kvb[po:po + 64, hd * 65:(hd + 1) * 65],
                            start=False, stop=True), inc=("s_t", 1))
                        pair_n[sub] = t_n
                    hd0 = 2 * hp
                    sbank = 6 + (hp % 2)
                    t_s2 = E("tensor", lambda h, c=c, hp=hp, sb_=sbank: h.matmul(
                        ps[sb_][:, 0:130], kt_r[:, c * 1024 + hp * 128: c * 1024 + hp * 128 + 128],
                        va_r[:, c * 1040 + hp * 130: c * 1040 + hp * 130 + 130],
                        start=True, stop=True),
                        waits=[bank_free[sbank]], inc=("s_t", 1))
                    t_kv = E("vector", lambda h, hp=hp, sb_=sbank: h.tensor_add(
                        kv[:, hp * 130:(hp + 1) * 130], kv[:, hp * 130:(hp + 1) * 130],
                        ps[sb_][:, 0:130]),
                        waits=[t_s2], inc=("s_v", 1))
                    bank_free[sbank] = t_kv
                    if hd0 + 4 < NH:
                        am_toks[hd0 + 4] = emit_A(c, hd0 + 4)
                        am_toks[hd0 + 5] = emit_A(c, hd0 + 5)
                    t_d = E("scalar", lambda h, nb=nbank: h.activation(
                        den[:, 0:2], ps[nb][:, 0:130].rearrange("p (hh z) -> p hh z", z=65)[:, :, 64:65]
                            .rearrange("p hh z -> p (hh z)"),
                        AF.Identity, bias=epsc[:, 1:2]),
                        waits=[pair_n[1]], inc=("s_s", 1))
                    t_rc = E("vector", lambda h: h.reciprocal(rcp[:, 0:2], den[:, 0:2]),
                             waits=[t_d], inc=("s_v", 1))
                    t_o = E("scalar", lambda h, nb=nbank, c=c, hd=hd0: h.activation(
                        cast[:, c * 1024 + hd * 64: c * 1024 + (hd + 1) * 64],
                        ps[nb][:, 0:64], AF.Identity, scale=rcp[:, 0:1]),
                        waits=[t_rc], inc=("s_s", 1))
                    t_o = E("scalar", lambda h, nb=nbank, c=c, hd=hd0 + 1: h.activation(
                        cast[:, c * 1024 + hd * 64: c * 1024 + (hd + 1) * 64],
                        ps[nb][:, 65:129], AF.Identity, scale=rcp[:, 1:2]),
                        inc=("s_s", 1))
                    o_tok[(c, hp)] = t_o
                    bank_free[nbank] = t_o
                    drain_mix(2)
                    if hp % 2 == 1:
                        drain_side(1)
                drain_side(99)
                kv_ready = E("scalar", lambda h: h.activation(kvb[:, :], kv[:, :], AF.Identity),
                             waits=[P.tok("s_v"), P.tok("s_t")], inc=("s_s", 1))
                queue_mix(2 * c)
                queue_mix(2 * c + 1)
            t_att_T = P.tok("s_t")
            t_att_S = P.tok("s_s")

            oT_tok = {}
            for eg in range(8):
                bk = nbig()
                for tau in range(4):
                    def mm_ot(h, bk=bk, tau=tau, eg=eg):
                        return h.matmul(ps[bk][:, tau * 128:(tau + 1) * 128],
                                        cast[:, tau * 1024 + eg * 128: tau * 1024 + eg * 128 + 128],
                                        idb[:, :], start=True, stop=True)
                    E("tensor", mm_ot, waits=[o_tok[(3, eg)], bank_free[bk]],
                      inc=("s_t", 1) if tau == 3 else None)
                t_mm = P.tok("s_t")
                oT_tok[eg] = E("vector", lambda h, bk=bk, eg=eg: h.tensor_copy(
                    hT[:, eg * GT:(eg + 1) * GT], ps[bk][:, 0:512]),
                    waits=[t_mm], inc=("s_v", 1))
                bank_free[bk] = oT_tok[eg]
                drain_mix(2)
            t_oT = P.tok("s_v")
            drain_mix(99)
            t_mixdone = P.tok("s_v")
            t_carry = E("vector", lambda h: h.tensor_copy(
                carry[:, :], xn[:].rearrange("p (k t) -> p k t", t=GT)[:, :, GT - 1:GT]),
                waits=[t_mixdone], inc=("s_v", 1))

            x1_tok = {}
            cm_tok = {}
            sq_tok = {}
            for m in range(8):
                bk = nbig()
                for kt in range(8):
                    def mm_wo(h, bk=bk, kt=kt, m=m):
                        return h.matmul(ps[bk][:, 0:512],
                                        wo[:, kt * 1024 + m * 128: kt * 1024 + (m + 1) * 128],
                                        hT[:, kt * GT:(kt + 1) * GT],
                                        start=(kt == 0), stop=(kt == 7))
                    E("tensor", mm_wo, waits=[oT_tok[kt], bank_free[bk], t_wstart],
                      inc=("s_t", 1) if kt == 7 else None)
                t_mm = P.tok("s_t")
                msl = slice(m * GT, (m + 1) * GT)
                x1_tok[m] = E("vector", lambda h, msl=msl, bk=bk: h.tensor_add(
                    xn[:, msl], xn[:, msl], ps[bk][:, 0:512]),
                    waits=[t_mm, t_carry], inc=("s_v", 1))
                bank_free[bk] = x1_tok[m]
                cm_tok[m] = E("scalar", lambda h, msl=msl: h.activation(
                    cast[:, msl], xn[:, msl], AF.Identity),
                    waits=[x1_tok[m], P.tok("s_t") if m == 0 else None], inc=("s_s", 1))
                sq_tok[m] = E("vector", lambda h, msl=msl: h.tensor_mul(
                    u[:, msl], cast[:, msl], cast[:, msl]),
                    waits=[cm_tok[m]], inc=("s_v", 1))
            t_x1 = P.tok("s_v")
            def rwkv_stream(mat_i, m, g):
                idx = mat_i * 8 + m
                slot = rwkv_stream.c % 4
                rwkv_stream.c += 1
                wsrc = (wrT, wkT2, wvT2)[mat_i]
                ring_dst = wr_ring[:, slot * 1024:(slot + 1) * 1024]
                t_blk = stream(ring_dst.rearrange("p (k c) -> p k c", c=128),
                               wsrc.rearrange("(k p) c -> p k c", p=128)[:, :, m * 128:(m + 1) * 128],
                               f"s_wr{slot}",
                               waits=[wr_last[slot]],
                               scratch=("r", idx, wsc_r[idx]), g=g, flat=ring_dst)
                return slot, ring_dst, t_blk
            rwkv_stream.c = 0

            def rwkv_mms(slot, ring_dst, t_blk, rhs_fn, extra=()):
                bk = nbig()
                for kt in range(8):
                    def mm_rwkv(h, bk=bk, kt=kt, rd=ring_dst, rf=rhs_fn):
                        return h.matmul(ps[bk][:, 0:512], rd[:, kt * 128:(kt + 1) * 128],
                                        rf(kt), start=(kt == 0), stop=(kt == 7))
                    E("tensor", mm_rwkv, waits=[t_blk, bank_free[bk]] + list(extra),
                      inc=("s_t", 1) if kt == 7 else None)
                t_mm = P.tok("s_t")
                wr_last[slot] = t_mm
                return bk, t_mm

            for m in range(8):
                slot, rd, t_blk = rwkv_stream(1, m, g)
                bk, t_mm = rwkv_mms(slot, rd, t_blk, lambda kt: xk_r[:, kt * GT:(kt + 1) * GT],
                                    extra=[t_mixdone])
                msl = slice(m * GT, (m + 1) * GT)
                t_rl = E("scalar", lambda h, bk=bk: h.activation(
                    tmp[:, 0:512], ps[bk][:, 0:512], AF.Relu),
                    waits=[t_mm, P.tok("s_v")], inc=("s_s", 1))
                bank_free[bk] = E("vector", lambda h, msl=msl: h.tensor_mul(
                    hT[:, msl], tmp[:, 0:512], tmp[:, 0:512]),
                    waits=[t_rl], inc=("s_v", 1))
            t_kact = P.tok("s_v")

            for m in range(8):
                E("tensor", lambda h, m=m: h.matmul(
                    ps[4][0:1, 0:512], onesb[:, 0:1], cast[:, m * GT:(m + 1) * GT],
                    start=(m == 0), stop=(m == 7)),
                    waits=[cm_tok[m], bank_free[4] if m == 0 else None],
                    inc=("s_t", 1) if m == 7 else None)
            for m in range(8):
                E("tensor", lambda h, m=m: h.matmul(
                    ps[5][0:1, 0:512], onesb[:, 0:1], u[:, m * GT:(m + 1) * GT],
                    start=(m == 0), stop=(m == 7)),
                    waits=[sq_tok[m], bank_free[5] if m == 0 else None],
                    inc=("s_t", 1) if m == 7 else None)
            t_stats2b = P.tok("s_t")
            E("vector", lambda h: h.tensor_scalar_mul(rM, ps[4][0:1, 0:512], 1.0 / D),
              waits=[t_stats2b], inc=("s_v", 1))
            E("vector", lambda h: h.tensor_mul(rV, rM, rM), inc=("s_v", 1))
            t_mrowb = E("vector", lambda h: h.scalar_tensor_tensor(rV, ps[5][0:1, 0:512], 1.0 / D, rV,
                                                                  ALU.mult, ALU.subtract),
                        inc=("s_v", 1))
            bank_free[4] = t_mrowb
            bank_free[5] = t_mrowb
            t_sqrb = E("scalar", lambda h: h.activation(rR, rV, AF.Sqrt, bias=epsc[0:1, 0:1]),
                       waits=[t_mrowb], inc=("s_s", 1))
            t_rstdb = E("vector", lambda h: h.reciprocal(rR, rR), waits=[t_sqrb], inc=("s_v", 1))
            r_read = {-2: r_read_last[0], -1: r_read_last[1]}
            r_tok = {}
            for m in range(8):
                if m == 2:
                    t_mbcb = E("tensor", lambda h: h.matmul(ps[2][:, 0:512], ones128, rM,
                                                            start=True, stop=True),
                               waits=[t_rstdb, bank_free[2]], inc=("s_t", 1))
                    t_rbcb = E("tensor", lambda h: h.matmul(ps[3][:, 0:512], ones128, rR,
                                                            start=True, stop=True),
                               waits=[bank_free[3]], inc=("s_t", 1))
                    h2_tok = {}
                    for kt in range(8):
                        sl = slice(kt * GT, (kt + 1) * GT)
                        E("vector", lambda h, sl=sl: h.tensor_sub(cast[:, sl], xn[:, sl], ps[2][:, 0:512]),
                          waits=[t_mbcb, t_stats2b, t_rbcb], inc=("s_v", 1))
                        E("vector", lambda h, sl=sl: h.tensor_mul(cast[:, sl], cast[:, sl], ps[3][:, 0:512]),
                          inc=("s_v", 1))
                        h2_tok[kt] = E("vector", lambda h, sl=sl, kt=kt: h.tensor_scalar(
                            cast[:, sl], cast[:, sl], g2c[:, kt:kt + 1], be2c[:, kt:kt + 1], ALU.mult, ALU.add),
                          inc=("s_v", 1))
                    t_h2 = P.tok("s_v")
                    bank_free[2] = t_h2
                    bank_free[3] = t_h2
                slot, rd, t_blk = rwkv_stream(0, m, g)
                bk, t_mm = rwkv_mms(slot, rd, t_blk, lambda kt: xr_r[:, kt * GT:(kt + 1) * GT],
                                    extra=[t_mixdone])
                rsl = slice((m % 2) * 512, (m % 2) * 512 + 512)
                r_tok[m] = E("scalar", lambda h, bk=bk, rsl=rsl: h.activation(
                    r_ring[:, rsl], ps[bk][:, 0:512], AF.Sigmoid),
                    waits=[t_mm, r_read[m - 2]], inc=("s_s", 1))
                bank_free[bk] = r_tok[m]
                slot2, rd2, t_blk2 = rwkv_stream(2, m, g)
                bk2, t_mm2 = rwkv_mms(slot2, rd2, t_blk2, lambda kt: hT[:, kt * GT:(kt + 1) * GT],
                                      extra=[t_kact])
                msl = slice(m * GT, (m + 1) * GT)
                t_p = E("vector", lambda h, rsl=rsl, bk2=bk2: h.tensor_tensor(
                    tmp[:, 0:512], ps[bk2][:, 0:512], r_ring[:, rsl], ALU.mult),
                    waits=[t_mm2, r_tok[m]], inc=("s_v", 1))
                bank_free[bk2] = t_p
                r_read[m] = E("vector", lambda h, msl=msl, x=xg: h.tensor_add(
                    x[:, msl], x[:, msl], tmp[:, 0:512]),
                    waits=[t_p], inc=("s_v", 1))
            r_read_last = [r_read[6], r_read[7]]
            t_rwkv = P.tok("s_v")
            t_acc1 = E("vector", lambda h, x=xg: h.tensor_add(x[:, :], x[:, :], xn[:, :]),
                       waits=[t_x1], inc=("s_v", 1))

            gelu_toks = {}
            for f in range(32):
                slot = f % 4
                ring_dst = w1_ring[:, slot * 1024:(slot + 1) * 1024]
                t_blk = stream(ring_dst.rearrange("p (k c) -> p k c", c=128),
                               w1_d.rearrange("(k p) c -> p k c", p=128)[:, :, f * 128:(f + 1) * 128],
                               f"s_w1{slot}", waits=[w1_last[slot]],
                               scratch=("1", f, wsc_1[f]), g=g, flat=ring_dst,
                               q2="sync" if slot % 2 == 0 else "gpsimd")
                bk = nbig()
                for kt in range(8):
                    def mm_1(h, bk=bk, kt=kt, rd=ring_dst):
                        return h.matmul(ps[bk][:, 0:512], rd[:, kt * 128:(kt + 1) * 128],
                                        cast[:, kt * GT:(kt + 1) * GT],
                                        start=(kt == 0), stop=(kt == 7))
                    E("tensor", mm_1, waits=[t_blk, h2_tok[kt], bank_free[bk], t_att_T],
                      inc=("s_t", 1) if kt == 7 else None)
                t_mm = P.tok("s_t")
                w1_last[slot] = t_mm
                tk = E("scalar", lambda h, f=f, bk=bk: h.activation(
                    y1_r[:, f * GT:(f + 1) * GT], ps[bk][:, 0:512], AF.Gelu, bias=b1c[:, f:f + 1]),
                    waits=[t_mm, t_vec], inc=("s_s", 1))
                bank_free[bk] = tk
                gelu_toks[f] = tk
            t_y1 = P.tok("s_s")
            t_mlp1_T = P.tok("s_t")

            if g + 1 < n_groups:
                t_x_next = emit_xload(g + 1)

            mlp2_state = {"m": 0}

            def mlp2_portion(upto):
                while mlp2_state["m"] < upto:
                    m = mlp2_state["m"]
                    mlp2_state["m"] += 1
                    msl = slice(m * GT, (m + 1) * GT)
                    bk = nbig()
                    for hf in range(2):
                        idx = m * 2 + hf
                        slot = idx % 3
                        ring_dst = w2_ring[:, slot * 2048:(slot + 1) * 2048]
                        t_blk = stream(ring_dst.rearrange("p (k c) -> p k c", c=128),
                                       w2_d.rearrange("(k p) c -> p k c", p=128)[:, hf * 16:(hf + 1) * 16,
                                                                                 m * 128:(m + 1) * 128],
                                       f"s_w2{slot}", waits=[w2_last[slot]],
                                       scratch=("2", idx, wsc_2[idx]), g=g, flat=ring_dst,
                                       q2="gpsimd" if slot == 2 else "sync")
                        for kt in range(16):
                            gk = hf * 16 + kt
                            def mm_2(h, bk=bk, kt=kt, gk=gk, rd=ring_dst):
                                return h.matmul(ps[bk][:, 0:512], rd[:, kt * 128:(kt + 1) * 128],
                                                y1_r[:, gk * GT:(gk + 1) * GT],
                                                start=(gk == 0), stop=(gk == 31))
                            E("tensor", mm_2, waits=[t_blk, gelu_toks[gk], bank_free[bk]],
                              inc=("s_t", 1) if kt == 15 else None)
                        w2_last[slot] = P.tok("s_t")
                    t_mm = P.tok("s_t")
                    bank_free[bk] = E("vector", lambda h, msl=msl, bk=bk, m=m, x=xg: h.scalar_tensor_tensor(
                        x[:, msl], ps[bk][:, 0:512], b2c[:, m:m + 1], x[:, msl], ALU.add, ALU.add),
                        waits=[t_mm, t_acc1], inc=("s_v", 1))

            if g + 1 < n_groups:
                ln1_toks = emit_ln1(g + 1, t_x_next, t_mlp1_T,
                                    mlp2_hook=mlp2_portion)
                mlp2_portion(8)
            else:
                mlp2_portion(8)
            t_mlp = P.tok("s_v")
            mlp2_done = P.tok("s_t")

            t_out = E("scalar", lambda h, x=xg, b=batch, gs=gslice: h.dma_start(
                out=oT.rearrange("b (k p) t -> b p k t", p=128)[b:b + 1, :, :, gs]
                      .rearrange("b p k t -> p (b k) t"),
                in_=x[:].rearrange("p (k t) -> p k t", t=GT)),
                waits=[t_mlp], inc=(f"s_o{g % 2}", 16))
            out_toks.append(t_out)

        E("sync", lambda h: h.nop(), waits=[P.tok("s_o0"), P.tok("s_o1")])

        @block.tensor
        def _(t):
            P.run("tensor", t, sems)

        @block.vector
        def _(v):
            P.run("vector", v, sems)

        @block.scalar
        def _(s):
            P.run("scalar", s, sems)

        @block.sync
        def _(s):
            P.run("sync", s, sems)

        @block.gpsimd
        def _(g):
            P.run("gpsimd", g, sems)

    return nc


def kernel(L0_attn_x, L0_attn_wq, L0_attn_wk, L0_attn_wv, L0_attn_wo, L0_attn_proj,
           L0_attn_w1, L0_attn_b1, L0_attn_w2, L0_attn_b2, L0_attn_g1, L0_attn_beta1,
           L0_attn_g2, L0_attn_beta2, L0_ffn_W_r, L0_ffn_W_k, L0_ffn_W_v,
           L0_ffn_mu_r, L0_ffn_mu_k):
    f32 = np.float32
    x = np.ascontiguousarray(np.asarray(L0_attn_x, dtype=f32))
    n_cores = 8

    asc = lambda a: np.ascontiguousarray(np.asarray(a, dtype=f32))
    wq, wk, wv, wo = asc(L0_attn_wq), asc(L0_attn_wk), asc(L0_attn_wv), asc(L0_attn_wo)
    proj = asc(L0_attn_proj)
    w1, w2 = asc(L0_attn_w1), asc(L0_attn_w2)
    b1, b2 = asc(L0_attn_b1), asc(L0_attn_b2)
    g1, be1, g2, be2 = asc(L0_attn_g1), asc(L0_attn_beta1), asc(L0_attn_g2), asc(L0_attn_beta2)
    Wr, Wk2, Wv2 = asc(L0_ffn_W_r), asc(L0_ffn_W_k), asc(L0_ffn_W_v)
    mur, muk = asc(L0_ffn_mu_r), asc(L0_ffn_mu_k)

    pblk = np.zeros((128, 128), f32)
    pblk[0:64, 0:64] = proj
    pblk[64:128, 64:128] = proj
    cols = lambda v: np.ascontiguousarray(v.reshape(-1, 128).T)
    vecs = np.zeros((128, 64), f32)
    vecs[:, 0:8] = cols(g1)
    vecs[:, 8:16] = cols(be1)
    vecs[:, 16:24] = cols(g2)
    vecs[:, 24:32] = cols(be2)
    vecs[:, 32:40] = cols(mur)
    vecs[:, 40:48] = cols(muk)
    vecs[:, 48:56] = cols(b2)
    b1c = cols(b1)
    maskc = np.ascontiguousarray(np.tile(np.triu(np.ones((128, 128), f32)), (1, 4)))
    idc = np.eye(128, dtype=f32)

    shared = {
        "wqT": np.ascontiguousarray(wq.T), "wkT": np.ascontiguousarray(wk.T),
        "wv": wv, "wo": wo, "pblk": pblk,
        "wrT": np.ascontiguousarray(Wr.T), "wkT2": np.ascontiguousarray(Wk2.T),
        "wvT2": np.ascontiguousarray(Wv2.T),
        "w1": w1, "w2": w2, "vecs": vecs, "b1c": b1c, "maskc": maskc, "idc": idc,
    }
    in_maps = []
    for c in range(n_cores):
        xc = np.ascontiguousarray(np.swapaxes(x[2 * c:2 * c + 2], 1, 2))
        m = dict(shared)
        m["xT"] = xc
        in_maps.append(m)

    nc = build_nc()
    res = run_bass_kernel_spmd(nc, in_maps, core_ids=list(range(n_cores)))
    outs = []
    for c in range(n_cores):
        outs.append(np.swapaxes(res.results[c]["oT"], 1, 2))
    return np.ascontiguousarray(np.concatenate(outs, axis=0))


# version 50
# speedup vs baseline: 1.0127x; 1.0127x over previous
import numpy as np
import ml_dtypes
from contextlib import ExitStack

import concourse.bass as bass
import concourse.mybir as mybir
from concourse.bass_utils import run_bass_kernel_spmd

F32 = mybir.dt.float32
BF16 = mybir.dt.bfloat16
AF = mybir.ActivationFunctionType
ALU = mybir.AluOpType

D = 1024
FF = 4096
NH = 16
DH = 64
SEQ = 2048
GT = 512
NG = 8
NCHUNK = 4
EPS_LN = 1e-5
EPS_ATT = 1e-6


class Prog:
    ENGS = ("tensor", "vector", "scalar", "sync", "gpsimd")

    def __init__(self):
        self.ops = {e: [] for e in self.ENGS}
        self.count = {}
        self.waited = {e: {} for e in self.ENGS}

    def tok(self, sem):
        return (sem, self.count.get(sem, 0))

    def emit(self, eng, fn, waits=(), inc=None):
        ws = []
        for w in waits:
            if w is None:
                continue
            s, v = w
            if v > 0 and self.waited[eng].get(s, 0) < v:
                self.waited[eng][s] = v
                ws.append((s, v))
        tok = None
        if inc is not None:
            s, k = inc
            self.count[s] = self.count.get(s, 0) + k
            tok = (s, self.count[s])
        self.ops[eng].append((ws, fn, inc))
        return tok

    def run(self, eng_name, handle, sems):
        for ws, fn, inc in self.ops[eng_name]:
            for s, v in ws:
                handle.wait_ge(sems[s], v)
            inst = fn(handle)
            if inc is not None:
                s, k = inc
                inst.then_inc(sems[s], k)


def build_nc(n_groups=NG):
    nc = bass.Bass("TRN2", target_bir_lowering=False, detect_race_conditions=False)

    ap = lambda name, shape, dt: nc.dram_tensor(name, shape, dt, kind="ExternalInput").ap()
    xT = ap("xT", [2, D, SEQ], F32)
    wqT = ap("wqT", [D, D], F32)
    wkT = ap("wkT", [D, D], F32)
    wv_d = ap("wv", [D, D], F32)
    wo_d = ap("wo", [D, D], F32)
    pblk = ap("pblk", [128, 128], F32)
    wrT = ap("wrT", [D, D], F32)
    wkT2 = ap("wkT2", [D, D], F32)
    wvT2 = ap("wvT2", [D, D], F32)
    w1_d = ap("w1", [D, FF], F32)
    w2_d = ap("w2", [FF, D], F32)
    vecs = ap("vecs", [128, 64], F32)
    b1c_d = ap("b1c", [128, 32], F32)
    maskc = ap("maskc", [128, 512], F32)
    idc = ap("idc", [128, 128], F32)
    oT = nc.dram_tensor("oT", [2, D, SEQ], F32, kind="ExternalOutput").ap()
    wsc_r = nc.dram_tensor("wsc_r", [24, 128, 1024], BF16, kind="Internal").ap()
    wsc_1 = nc.dram_tensor("wsc_1", [32, 128, 1024], BF16, kind="Internal").ap()
    wsc_2 = nc.dram_tensor("wsc_2", [16, 128, 2048], BF16, kind="Internal").ap()

    P = Prog()
    E = P.emit

    with ExitStack() as es:
        sb = lambda name, shape, dt: es.enter_context(nc.sbuf_tensor(name, shape, dt))
        wqp = sb("wqp", [128, 8 * D], BF16)
        wkp = sb("wkp", [128, 8 * D], BF16)
        wv = sb("wv_sb", [128, 8 * D], BF16)
        xbuf = [sb("xa", [128, 8 * GT], F32), sb("xb2", [128, 8 * GT], F32)]
        xn = sb("xn", [128, 8 * GT], F32)
        cast = sb("cast", [128, 8 * GT], BF16)
        hT = sb("hT", [128, 8 * GT], BF16)
        u = sb("u", [128, 16448], BF16)
        tmp = sb("tmp", [128, GT + 1], F32)
        am = sb("am", [128, 512], BF16)
        wo = sb("wo_sb", [128, 8 * D], BF16)
        den = sb("den", [128, 2], F32)
        rcp = sb("rcp", [128, 2], F32)
        kv = sb("kv", [128, NH * 65], F32)
        kvb = sb("kvb", [128, NH * 65], BF16)
        carry = sb("carry", [128, 8], F32)
        rows = sb("rows", [128, 3 * GT + 128], F32)
        vcol = sb("vcol", [128, 64], F32)
        b1c = sb("b1c_sb", [128, 32], F32)
        maskb = sb("maskb", [128, 512], BF16)
        idb = sb("idb", [128, 128], BF16)
        onesb = sb("onesb", [128, 1], BF16)
        epsc = sb("epsc", [128, 2], F32)
        wr_ring = sb("wr_ring", [128, 4 * 1024], BF16)
        r_ring = sb("r_ring", [128, 2 * 512], BF16)
        w1_ring = sb("w1_ring", [128, 4 * 1024], BF16)
        w2_ring = sb("w2_ring", [128, 3 * 2048], BF16)

        ps = [es.enter_context(nc.psum_tensor(f"ps{i}", [128, 512], F32)) for i in range(8)]
        sems = {}
        for s in (["s_x", "s_o", "s_w", "s_t", "s_v", "s_s"]
                  + [f"s_wr{i}" for i in range(4)]
                  + [f"s_w1{i}" for i in range(4)]
                  + [f"s_w2{i}" for i in range(3)] + ["s_o0", "s_o1"]
                  + [f"s_wr{i}h" for i in range(4)]
                  + [f"s_w1{i}h" for i in range(4)]
                  + [f"s_w2{i}h" for i in range(3)]
                  + ["s_sc", "s_g"]):
            sems[s] = es.enter_context(nc.semaphore(s))
        block = es.enter_context(nc.Block())

        e1 = am
        q_r = u[:, 0:4096]
        k_r = u[:, 4096:8192]
        kt_r = u[:, 8192:12288]
        va_r = u[:, 12288:16448]
        y1_r = u[:, 0:16384]
        xr_r = kt_r
        xk_r = u[:, 12288:16384]
        rM = rows[0:1, 0:GT]
        rV = rows[0:1, GT:2 * GT]
        rR = rows[0:1, 2 * GT:3 * GT]
        ones128 = rows[0:1, 3 * GT:3 * GT + 128]

        g1c = vcol[:, 0:8]
        be1c = vcol[:, 8:16]
        g2c = vcol[:, 16:24]
        be2c = vcol[:, 24:32]
        murc = vcol[:, 32:40]
        mukc = vcol[:, 40:48]
        b2c = vcol[:, 48:56]

        def _ld(dst, src):
            return lambda h, d=dst, s=src: h.dma_start(out=d, in_=s)

        E("sync", _ld(vcol[:, :], vecs), inc=("s_x", 16))
        E("sync", _ld(b1c[:, :], b1c_d), inc=("s_x", 16))
        t_vec = P.tok("s_x")
        E("sync", _ld(xbuf[0][:].rearrange("p (k t) -> p k t", t=GT),
                      xT.rearrange("b (k p) t -> b p k t", p=128)[0:1, :, :, 0:GT]
                        .rearrange("b p k t -> p (b k) t")),
          inc=("s_x", 16))
        t_x0 = P.tok("s_x")

        E("gpsimd", _ld(u[:, 0:8192].rearrange("p (k c) -> p k c", c=1024),
                        wqT.rearrange("(k p) c -> p k c", p=128)), inc=("s_w", 16))
        E("gpsimd", _ld(u[:, 8192:16384].rearrange("p (k c) -> p k c", c=1024),
                        wkT.rearrange("(k p) c -> p k c", p=128)), inc=("s_w", 16))
        E("gpsimd", _ld(am[:, 0:128], pblk), inc=("s_w", 16))
        t_foldw = P.tok("s_w")
        E("gpsimd", _ld(wv[:].rearrange("p (k c) -> p k c", c=1024),
                        wv_d.rearrange("(k p) c -> p k c", p=128)), inc=("s_w", 16))
        t_wv = P.tok("s_w")
        E("gpsimd", _ld(maskb[:, :], maskc), inc=("s_w", 16))
        t_maskb = P.tok("s_w")
        E("gpsimd", _ld(idb[:, :], idc), inc=("s_w", 16))
        t_idb = P.tok("s_w")
        E("gpsimd", _ld(wo[:].rearrange("p (k c) -> p k c", c=1024),
                        wo_d.rearrange("(k p) c -> p k c", p=128)), inc=("s_w", 16))
        t_wstart = P.tok("s_w")

        E("vector", lambda h: h.memset(onesb[:, :], 1.0), inc=("s_v", 1))
        E("vector", lambda h: h.memset(epsc[:, 0:1], EPS_LN), inc=("s_v", 1))
        E("vector", lambda h: h.memset(epsc[:, 1:2], EPS_ATT), inc=("s_v", 1))
        E("vector", lambda h: h.memset(ones128, 1.0), inc=("s_v", 1))
        E("vector", lambda h: h.memset(kv[:, :], 0.0), inc=("s_v", 1))
        E("vector", lambda h: h.memset(kvb[:, :], 0.0), inc=("s_v", 1))
        E("vector", lambda h: h.memset(carry[:, :], 0.0), inc=("s_v", 1))
        t_const = P.tok("s_v")

        fb_free = {2: None, 3: None}
        for dst, stof in ((wqp, 0), (wkp, 8192)):
            for m in range(8):
                for jg in range(2):
                    bank = 2 + jg
                    for j4 in range(4):
                        j = jg * 4 + j4
                        def mm_fold(h, b=bank, j4=j4, j=j, m=m, so=stof):
                            return h.matmul(ps[b][:, j4 * 128:(j4 + 1) * 128],
                                            u[:, so + j * 1024 + m * 128: so + j * 1024 + (m) * 128 + 128],
                                            am[:, 0:128], start=True, stop=True)
                        tk = E("tensor", mm_fold,
                               waits=[t_foldw, t_const, fb_free[bank]],
                               inc=("s_t", 1) if j4 == 3 else None)
                    def ev_fold(h, b=bank, d=dst, m=m, jg=jg):
                        return h.tensor_copy(d[:, m * 1024 + jg * 512: m * 1024 + (jg + 1) * 512],
                                             ps[b][:, 0:512])
                    fb_free[bank] = E("vector", ev_fold, waits=[tk], inc=("s_v", 1))
        t_fold = P.tok("s_v")

        sc_written = {}
        slot_sc = {}

        def stream(dst_ap, src_ap, sem, waits=(), scratch=None, g=0, flat=None, q2=None):
            if scratch is None or g == 0:
                def fn(h, d=dst_ap, s=src_ap):
                    return h.dma_start(out=d, in_=s)
                E("gpsimd", fn, waits=list(waits) + [slot_sc.get(sem)], inc=(sem, 16))
                tok = P.tok(sem)
                if scratch is not None:
                    kind, blk, sc_ap = scratch
                    def wr(h, d=sc_ap, s=flat):
                        return h.dma_start(out=d, in_=s)
                    E("scalar", wr, waits=[tok], inc=("s_sc", 16))
                    sc_written[(kind, blk)] = P.tok("s_sc")
                    slot_sc[sem] = sc_written[(kind, blk)]
                return tok
            else:
                kind, blk, sc_ap = scratch
                q = q2 or ("gpsimd" if kind == "r" else "sync")
                def fn(h, d=flat, s=sc_ap):
                    return h.dma_start(out=d, in_=s)
                E(q, fn, waits=list(waits) + [sc_written[(kind, blk)]], inc=(sem + "h", 16))
                return P.tok(sem + "h")

        bank_free = {i: None for i in range(8)}
        bank_free[0] = t_const
        bank_free[1] = t_const
        bank_free[2] = fb_free[2]
        bank_free[3] = fb_free[3]
        big_free = bank_free
        _bigc = [0]

        def nbig():
            b = (0, 1, 6, 7)[_bigc[0] % 4]
            _bigc[0] += 1
            return b

        _bigc6 = [0]

        def nbig6():
            b = (0, 1, 6, 7, 2, 3)[_bigc6[0] % 6]
            _bigc6[0] += 1
            return b

        out_toks = []
        kv_ready = t_const
        wr_last = [t_fold, t_fold]
        w1_last = [t_fold] * 4
        w2_last = [t_fold] * 3

        wr_last = [t_fold] * 4

        def emit_xload(g):
            batch, c4 = g // 4, g % 4
            gslice = slice(c4 * GT, (c4 + 1) * GT)
            xg = xbuf[g % 2]
            w = [out_toks[g - 2]] if g >= 2 else []
            return E("scalar", _ld(xg[:].rearrange("p (k t) -> p k t", t=GT),
                                 xT.rearrange("b (k p) t -> b p k t", p=128)[batch:batch + 1, :, :, gslice]
                                   .rearrange("b p k t -> p (b k) t")),
                     waits=w, inc=("s_x", 16))

        def emit_ln1(g, t_x, t_mlp1_prev, mlp2_hook=None):
            xg = xbuf[g % 2]
            xb_tok = {}
            for kt in range(8):
                sl = slice(kt * GT, (kt + 1) * GT)
                xb_tok[kt] = E("scalar", lambda h, x=xg, sl=sl: h.activation(
                    cast[:, sl], x[:, sl], AF.Identity),
                    waits=[t_x, t_mlp1_prev], inc=("s_s", 1))
            t_xb = P.tok("s_s")
            if mlp2_hook: mlp2_hook(2)
            for kt in range(8):
                def mm_s1(h, kt=kt):
                    return h.matmul(ps[4][0:1, 0:512], onesb[:, 0:1],
                                    cast[:, kt * GT:(kt + 1) * GT],
                                    start=(kt == 0), stop=(kt == 7))
                E("tensor", mm_s1, waits=[xb_tok[kt], bank_free[4], t_fold],
                  inc=("s_t", 1) if kt == 7 else None)
            t_stats1 = P.tok("s_t")
            t_sq = E("vector", lambda h: h.tensor_mul(cast[:, :], cast[:, :], cast[:, :]),
                     waits=[t_stats1], inc=("s_v", 1))
            if mlp2_hook: mlp2_hook(4)
            for kt in range(8):
                def mm_s2(h, kt=kt):
                    return h.matmul(ps[5][0:1, 0:512], onesb[:, 0:1],
                                    cast[:, kt * GT:(kt + 1) * GT],
                                    start=(kt == 0), stop=(kt == 7))
                E("tensor", mm_s2, waits=[t_sq, bank_free[5]],
                  inc=("s_t", 1) if kt == 7 else None)
            t_stats2 = P.tok("s_t")
            E("vector", lambda h: h.tensor_scalar_mul(rM, ps[4][0:1, 0:512], 1.0 / D),
              waits=[t_stats1], inc=("s_v", 1))
            E("vector", lambda h: h.tensor_mul(rV, rM, rM), inc=("s_v", 1))
            t_mrow = E("vector", lambda h: h.scalar_tensor_tensor(rV, ps[5][0:1, 0:512], 1.0 / D, rV,
                                                                 ALU.mult, ALU.subtract),
                       waits=[t_stats2], inc=("s_v", 1))
            bank_free[4] = t_mrow
            bank_free[5] = t_mrow
            t_sqr = E("scalar", lambda h: h.activation(rR, rV, AF.Sqrt, bias=epsc[0:1, 0:1]),
                      waits=[t_mrow], inc=("s_s", 1))
            t_rstd = E("vector", lambda h: h.reciprocal(rR, rR), waits=[t_sqr], inc=("s_v", 1))
            E("vector", lambda h: h.tensor_mul(rV, rV, rR), waits=[t_rstd], inc=("s_v", 1))
            t_v2 = E("vector", lambda h: h.tensor_mul(rV, rV, rR), inc=("s_v", 1))
            t_s2r = E("scalar", lambda h: h.activation(rV, rV, AF.Sqrt, bias=epsc[0:1, 0:1]),
                      waits=[t_v2], inc=("s_s", 1))
            t_r2 = E("vector", lambda h: h.reciprocal(rV, rV), waits=[t_s2r], inc=("s_v", 1))
            if mlp2_hook: mlp2_hook(6)
            t_mbc = E("tensor", lambda h: h.matmul(ps[2][:, 0:512], ones128, rM,
                                                   start=True, stop=True),
                      waits=[t_r2, bank_free[2]], inc=("s_t", 1))
            t_rbc = E("tensor", lambda h: h.matmul(ps[3][:, 0:512], ones128, rR,
                                                   start=True, stop=True),
                      waits=[bank_free[3]], inc=("s_t", 1))
            t_r2bc = E("tensor", lambda h: h.matmul(ps[4][:, 0:512], ones128, rV,
                                                    start=True, stop=True),
                       waits=[bank_free[4]], inc=("s_t", 1))
            for kt in range(8):
                sl = slice(kt * GT, (kt + 1) * GT)
                E("vector", lambda h, sl=sl, x=xg: h.tensor_sub(xn[:, sl], x[:, sl], ps[2][:, 0:512]),
                  waits=[t_mbc, t_x], inc=("s_v", 1))
            for kt in range(8):
                sl = slice(kt * GT, (kt + 1) * GT)
                E("vector", lambda h, sl=sl: h.tensor_mul(xn[:, sl], xn[:, sl], ps[3][:, 0:512]),
                  waits=[t_rbc], inc=("s_v", 1))
            t_xn = P.tok("s_v")
            if mlp2_hook: mlp2_hook(7)
            h_tok = {}
            for kt in range(8):
                sl = slice(kt * GT, (kt + 1) * GT)
                E("vector", lambda h, sl=sl: h.tensor_mul(hT[:, sl], xn[:, sl], ps[4][:, 0:512]),
                  waits=[t_r2bc, t_xn], inc=("s_v", 1))
                h_tok[kt] = E("vector", lambda h, sl=sl, kt=kt: h.tensor_scalar(
                    hT[:, sl], hT[:, sl], g1c[:, kt:kt + 1], be1c[:, kt:kt + 1], ALU.mult, ALU.add),
                  waits=[t_vec], inc=("s_v", 1))
            t_h = P.tok("s_v")
            bank_free[2] = t_xn
            bank_free[3] = t_xn
            bank_free[4] = t_h
            if mlp2_hook: mlp2_hook(8)
            return {"t_h": t_h, "t_xn": t_xn, "t_stats2": t_stats2, "h_tok": h_tok}

        ln1_toks = emit_ln1(0, t_x0, ("s_t", 0))
        out_toks = []
        kv_ready = t_const
        mlp2_done = ("s_t", 0)
        r_read_last = [("s_v", 0), ("s_v", 0)]
        phi_prev = [None, None]

        for g in range(n_groups):
            batch, c4 = g // 4, g % 4
            xg = xbuf[g % 2]
            gslice = slice(c4 * GT, (c4 + 1) * GT)
            t_h = ln1_toks["t_h"]
            h_tok = ln1_toks["h_tok"]
            t_xn = ln1_toks["t_xn"]
            t_stats2 = ln1_toks["t_stats2"]

            if g == 4:
                tA = P.tok("s_t")
                tS = P.tok("s_s")
                tV = P.tok("s_v")
                E("vector", lambda h: h.memset(kv[:, :], 0.0), waits=[tA, tS, tV], inc=("s_v", 1))
                E("vector", lambda h: h.memset(kvb[:, :], 0.0), inc=("s_v", 1))
                t_reset = P.tok("s_v")
                kv_ready = t_reset
            else:
                t_reset = None

            t_vaok = P.tok("s_t")
            t_vones = E("vector", lambda h: h.memset(
                va_r[:].rearrange("p (c hh o) -> p c hh o", hh=NH, o=65)[:, :, :, 64:65], 1.0),
                waits=[t_vaok, t_h], inc=("s_v", 1))
            e1bufs = (e1[:, 0:512], tmp[:, 0:512])

            def phi_evac(dst, msl, bk, wtok, idx):
                eb = e1bufs[idx % 2]
                w0 = [wtok, phi_prev[idx % 2]]
                if idx == 0:
                    w0 += [P.tok("s_t"), r_read_last[1]]
                t2 = E("scalar", lambda h, bk=bk, eb=eb: h.activation(eb, ps[bk][:, 0:512], AF.Exp),
                       waits=w0, inc=("s_s", 1))
                t1 = E("vector", lambda h, eb=eb: h.tensor_scalar_min(eb, eb, 1.0),
                       waits=[t2], inc=("s_v", 1))
                t3 = E("vector", lambda h, dst=dst, msl=msl, bk=bk, eb=eb: h.scalar_tensor_tensor(
                    dst[:, msl], ps[bk][:, 0:512], 0.0, eb, ALU.max, ALU.add),
                    waits=[t1], inc=("s_v", 1))
                phi_prev[idx % 2] = t3
                return t3

            for qi, (wmat, dstbuf) in enumerate(((wqp, q_r), (wkp, k_r))):
                for m in range(8):
                    bk = nbig6()
                    for kt in range(8):
                        def mm_q(h, bk=bk, kt=kt, m=m, w=wmat):
                            return h.matmul(ps[bk][:, 0:512],
                                            w[:, kt * 1024 + m * 128: kt * 1024 + (m + 1) * 128],
                                            hT[:, kt * GT:(kt + 1) * GT],
                                            start=(kt == 0), stop=(kt == 7))
                        E("tensor", mm_q, waits=[t_h, bank_free[bk], t_fold],
                          inc=("s_t", 1) if kt == 7 else None)
                    t_mm = P.tok("s_t")
                    bank_free[bk] = phi_evac(dstbuf, slice(m * GT, (m + 1) * GT), bk, t_mm,
                                             qi * 8 + m)
            t_qk = P.tok("s_v")


            evt_tok = {}

            def emit_V(tau, half, bk):
                for kt in range(8):
                    def mm_v(h, bk=bk, kt=kt, tau=tau, half=half):
                        return h.matmul(ps[bk][:, 0:512],
                                        hT[:, kt * GT + tau * 128: kt * GT + tau * 128 + 128],
                                        wv[:, kt * 1024 + half * 512: kt * 1024 + (half + 1) * 512],
                                        start=(kt == 0), stop=(kt == 7))
                    E("tensor", mm_v, waits=[bank_free[bk], t_vones, t_wv],
                      inc=("s_t", 1) if kt == 7 else None)
                t_mm = P.tok("s_t")
                def ev_v(h, bk=bk, tau=tau, half=half):
                    dst = va_r[:, tau * 1040 + half * 8 * 65: tau * 1040 + (half * 8 + 8) * 65]                         .rearrange("p (hh c) -> p hh c", c=65)[:, :, 0:64]
                    src = ps[bk][:, 0:512].rearrange("p (hh c) -> p hh c", c=64)
                    return h.tensor_copy(dst, src)
                bank_free[bk] = E("vector", ev_v, waits=[t_mm], inc=("s_v", 1))

            def emit_Kt(tau, eg, bk):
                for e4 in range(4):
                    et = eg * 4 + e4
                    def mm_t(h, bk=bk, e4=e4, et=et, tau=tau):
                        return h.matmul(ps[bk][:, e4 * 128:(e4 + 1) * 128],
                                        k_r[:, et * GT + tau * 128: et * GT + tau * 128 + 128],
                                        idb[:, :], start=True, stop=True)
                    E("tensor", mm_t, waits=[t_qk, bank_free[bk], t_idb],
                      inc=("s_t", 1) if e4 == 3 else None)
                t_mm = P.tok("s_t")
                def ev_t(h, bk=bk, tau=tau, eg=eg):
                    return h.activation(kt_r[:, tau * 1024 + eg * 512: tau * 1024 + (eg + 1) * 512],
                                        ps[bk][:, 0:512], AF.Identity)
                evt_tok[(tau, eg)] = E("scalar", ev_t, waits=[t_mm], inc=("s_s", 1))
                bank_free[bk] = evt_tok[(tau, eg)]

            for tau in range(4):
                for half in range(2):
                    emit_V(tau, half, nbig())
            for eg in range(2):
                emit_Kt(0, eg, nbig())

            def emit_A(c, hd):
                po = 64 * (hd % 2)
                et = hd // 2
                aslot = hd % 4
                abank = 2 + (hd % 2)
                ksl = slice(et * GT + c * 128, et * GT + c * 128 + 128)
                t_a = E("tensor", lambda h, po=po, ksl=ksl, ab=abank: h.matmul(
                    ps[ab][0:128, 0:128], k_r[po:po + 64, ksl], q_r[po:po + 64, ksl],
                    start=True, stop=True),
                    waits=[evt_tok[(c, hd // 8)], bank_free[abank]], inc=("s_t", 1))
                t_am = E("vector", lambda h, a=aslot, ab=abank: h.tensor_tensor(
                    am[:, a * 128:(a + 1) * 128], ps[ab][0:128, 0:128], maskb[:, 0:128], ALU.mult),
                    waits=[t_a, t_maskb], inc=("s_v", 1))
                bank_free[abank] = t_am
                return t_am

            if g == 4:
                E("vector", lambda h: h.memset(carry[:, :], 0.0), inc=("s_v", 1))

            mix_pend = []

            def queue_mix(kt):
                s0 = kt * GT
                tsb = P.tok("s_t")
                mix_pend.append(lambda tsb=tsb, s0=s0: E(
                    "vector", lambda h, s0=s0: h.tensor_sub(
                        tmp[:, 1:512], xn[:, s0 + 1:s0 + 512], xn[:, s0:s0 + 511]),
                    waits=[tsb, t_xn], inc=("s_v", 1)))
                mix_pend.append(lambda s0=s0, kt=kt: E(
                    "vector", lambda h: h.tensor_sub(
                        tmp[:, 0:1], xn[:, s0:s0 + 1], carry[:, kt:kt + 1]), inc=("s_v", 1)))
                mix_pend.append(lambda s0=s0, kt=kt: E(
                    "vector", lambda h: h.scalar_tensor_tensor(
                        xr_r[:, s0 + 1:s0 + 512], tmp[:, 1:512], murc[:, kt:kt + 1],
                        xn[:, s0:s0 + 511], ALU.mult, ALU.add), inc=("s_v", 1)))
                mix_pend.append(lambda s0=s0, kt=kt: E(
                    "vector", lambda h: h.scalar_tensor_tensor(
                        xr_r[:, s0:s0 + 1], tmp[:, 0:1], murc[:, kt:kt + 1],
                        carry[:, kt:kt + 1], ALU.mult, ALU.add), inc=("s_v", 1)))
                mix_pend.append(lambda s0=s0, kt=kt: E(
                    "vector", lambda h: h.scalar_tensor_tensor(
                        xk_r[:, s0 + 1:s0 + 512], tmp[:, 1:512], mukc[:, kt:kt + 1],
                        xn[:, s0:s0 + 511], ALU.mult, ALU.add), inc=("s_v", 1)))
                mix_pend.append(lambda s0=s0, kt=kt: E(
                    "vector", lambda h: h.scalar_tensor_tensor(
                        xk_r[:, s0:s0 + 1], tmp[:, 0:1], mukc[:, kt:kt + 1],
                        carry[:, kt:kt + 1], ALU.mult, ALU.add), inc=("s_v", 1)))

            def drain_mix(n):
                for _ in range(min(n, len(mix_pend))):
                    mix_pend.pop(0)()

            o_tok = {}
            side_pend = []

            def drain_side(n):
                for _ in range(min(n, len(side_pend))):
                    side_pend.pop(0)()
            for c in range(NCHUNK):
                if c + 1 < NCHUNK:
                    side_pend.append(lambda c1=c + 1: emit_Kt(c1, 0, 6))
                    side_pend.append(lambda c1=c + 1: emit_Kt(c1, 1, 7))
                am_toks = {}
                for hh in range(4):
                    am_toks[hh] = emit_A(c, hh)
                for hp in range(NH // 2):
                    nbank = (4, 5, 0, 1)[hp % 4]
                    pair_n = {}
                    for sub in range(2):
                        hd = 2 * hp + sub
                        po = 64 * (hd % 2)
                        et = hd // 2
                        sbank = 6 + (hd % 2)
                        ksl = slice(et * GT + c * 128, et * GT + c * 128 + 128)
                        vsl = slice(c * 1040 + hd * 65, c * 1040 + (hd + 1) * 65)
                        aslot = hd % 4
                        co = sub * 65
                        E("tensor", lambda h, a=aslot, nb=nbank, vsl=vsl, co=co: h.matmul(
                            ps[nb][0:128, co:co + 65], am[:, a * 128:(a + 1) * 128], va_r[:, vsl],
                            start=True, stop=False),
                            waits=[am_toks[hd], bank_free[nbank] if sub == 0 else None, kv_ready])
                        t_n = E("tensor", lambda h, po=po, ksl=ksl, nb=nbank, hd=hd, co=co: h.matmul(
                            ps[nb][0:128, co:co + 65], q_r[po:po + 64, ksl],
                            kvb[po:po + 64, hd * 65:(hd + 1) * 65],
                            start=False, stop=True), inc=("s_t", 1))
                        pair_n[sub] = t_n
                    hd0 = 2 * hp
                    sbank = 6 + (hp % 2)
                    t_s2 = E("tensor", lambda h, c=c, hp=hp, sb_=sbank: h.matmul(
                        ps[sb_][:, 0:130], kt_r[:, c * 1024 + hp * 128: c * 1024 + hp * 128 + 128],
                        va_r[:, c * 1040 + hp * 130: c * 1040 + hp * 130 + 130],
                        start=True, stop=True),
                        waits=[bank_free[sbank]], inc=("s_t", 1))
                    t_kv = E("vector", lambda h, hp=hp, sb_=sbank: h.tensor_add(
                        kv[:, hp * 130:(hp + 1) * 130], kv[:, hp * 130:(hp + 1) * 130],
                        ps[sb_][:, 0:130]),
                        waits=[t_s2], inc=("s_v", 1))
                    bank_free[sbank] = t_kv
                    if hd0 + 4 < NH:
                        am_toks[hd0 + 4] = emit_A(c, hd0 + 4)
                        am_toks[hd0 + 5] = emit_A(c, hd0 + 5)
                    t_d = E("scalar", lambda h, nb=nbank: h.activation(
                        den[:, 0:2], ps[nb][:, 0:130].rearrange("p (hh z) -> p hh z", z=65)[:, :, 64:65]
                            .rearrange("p hh z -> p (hh z)"),
                        AF.Identity, bias=epsc[:, 1:2]),
                        waits=[pair_n[1]], inc=("s_s", 1))
                    t_rc = E("vector", lambda h: h.reciprocal(rcp[:, 0:2], den[:, 0:2]),
                             waits=[t_d], inc=("s_v", 1))
                    t_o = E("scalar", lambda h, nb=nbank, c=c, hd=hd0: h.activation(
                        cast[:, c * 1024 + hd * 64: c * 1024 + (hd + 1) * 64],
                        ps[nb][:, 0:64], AF.Identity, scale=rcp[:, 0:1]),
                        waits=[t_rc], inc=("s_s", 1))
                    t_o = E("scalar", lambda h, nb=nbank, c=c, hd=hd0 + 1: h.activation(
                        cast[:, c * 1024 + hd * 64: c * 1024 + (hd + 1) * 64],
                        ps[nb][:, 65:129], AF.Identity, scale=rcp[:, 1:2]),
                        inc=("s_s", 1))
                    o_tok[(c, hp)] = t_o
                    bank_free[nbank] = t_o
                    drain_mix(2)
                drain_side(99)
                kv_ready = E("scalar", lambda h: h.activation(kvb[:, :], kv[:, :], AF.Identity),
                             waits=[P.tok("s_v"), P.tok("s_t")], inc=("s_s", 1))
                queue_mix(2 * c)
                queue_mix(2 * c + 1)
            t_att_T = P.tok("s_t")
            t_att_S = P.tok("s_s")

            oT_tok = {}
            for eg in range(8):
                bk = nbig()
                for tau in range(4):
                    def mm_ot(h, bk=bk, tau=tau, eg=eg):
                        return h.matmul(ps[bk][:, tau * 128:(tau + 1) * 128],
                                        cast[:, tau * 1024 + eg * 128: tau * 1024 + eg * 128 + 128],
                                        idb[:, :], start=True, stop=True)
                    E("tensor", mm_ot, waits=[o_tok[(3, eg)], bank_free[bk]],
                      inc=("s_t", 1) if tau == 3 else None)
                t_mm = P.tok("s_t")
                oT_tok[eg] = E("vector", lambda h, bk=bk, eg=eg: h.tensor_copy(
                    hT[:, eg * GT:(eg + 1) * GT], ps[bk][:, 0:512]),
                    waits=[t_mm], inc=("s_v", 1))
                bank_free[bk] = oT_tok[eg]
                drain_mix(2)
            t_oT = P.tok("s_v")
            drain_mix(99)
            t_mixdone = P.tok("s_v")
            t_carry = E("vector", lambda h: h.tensor_copy(
                carry[:, :], xn[:].rearrange("p (k t) -> p k t", t=GT)[:, :, GT - 1:GT]),
                waits=[t_mixdone], inc=("s_v", 1))

            x1_tok = {}
            cm_tok = {}
            sq_tok = {}
            for m in range(8):
                bk = nbig()
                for kt in range(8):
                    def mm_wo(h, bk=bk, kt=kt, m=m):
                        return h.matmul(ps[bk][:, 0:512],
                                        wo[:, kt * 1024 + m * 128: kt * 1024 + (m + 1) * 128],
                                        hT[:, kt * GT:(kt + 1) * GT],
                                        start=(kt == 0), stop=(kt == 7))
                    E("tensor", mm_wo, waits=[oT_tok[kt], bank_free[bk], t_wstart],
                      inc=("s_t", 1) if kt == 7 else None)
                t_mm = P.tok("s_t")
                msl = slice(m * GT, (m + 1) * GT)
                x1_tok[m] = E("vector", lambda h, msl=msl, bk=bk: h.tensor_add(
                    xn[:, msl], xn[:, msl], ps[bk][:, 0:512]),
                    waits=[t_mm, t_carry], inc=("s_v", 1))
                bank_free[bk] = x1_tok[m]
                cm_tok[m] = E("scalar", lambda h, msl=msl: h.activation(
                    cast[:, msl], xn[:, msl], AF.Identity),
                    waits=[x1_tok[m], P.tok("s_t") if m == 0 else None], inc=("s_s", 1))
                sq_tok[m] = E("vector", lambda h, msl=msl: h.tensor_mul(
                    u[:, msl], cast[:, msl], cast[:, msl]),
                    waits=[cm_tok[m]], inc=("s_v", 1))
            t_x1 = P.tok("s_v")
            def rwkv_stream(mat_i, m, g):
                idx = mat_i * 8 + m
                slot = rwkv_stream.c % 4
                rwkv_stream.c += 1
                wsrc = (wrT, wkT2, wvT2)[mat_i]
                ring_dst = wr_ring[:, slot * 1024:(slot + 1) * 1024]
                t_blk = stream(ring_dst.rearrange("p (k c) -> p k c", c=128),
                               wsrc.rearrange("(k p) c -> p k c", p=128)[:, :, m * 128:(m + 1) * 128],
                               f"s_wr{slot}",
                               waits=[wr_last[slot]],
                               scratch=("r", idx, wsc_r[idx]), g=g, flat=ring_dst)
                return slot, ring_dst, t_blk
            rwkv_stream.c = 0

            def rwkv_mms(slot, ring_dst, t_blk, rhs_fn, extra=()):
                bk = nbig()
                for kt in range(8):
                    def mm_rwkv(h, bk=bk, kt=kt, rd=ring_dst, rf=rhs_fn):
                        return h.matmul(ps[bk][:, 0:512], rd[:, kt * 128:(kt + 1) * 128],
                                        rf(kt), start=(kt == 0), stop=(kt == 7))
                    E("tensor", mm_rwkv, waits=[t_blk, bank_free[bk]] + list(extra),
                      inc=("s_t", 1) if kt == 7 else None)
                t_mm = P.tok("s_t")
                wr_last[slot] = t_mm
                return bk, t_mm

            for m in range(8):
                slot, rd, t_blk = rwkv_stream(1, m, g)
                bk, t_mm = rwkv_mms(slot, rd, t_blk, lambda kt: xk_r[:, kt * GT:(kt + 1) * GT],
                                    extra=[t_mixdone])
                msl = slice(m * GT, (m + 1) * GT)
                t_rl = E("scalar", lambda h, bk=bk: h.activation(
                    tmp[:, 0:512], ps[bk][:, 0:512], AF.Relu),
                    waits=[t_mm, P.tok("s_v")], inc=("s_s", 1))
                bank_free[bk] = E("vector", lambda h, msl=msl: h.tensor_mul(
                    hT[:, msl], tmp[:, 0:512], tmp[:, 0:512]),
                    waits=[t_rl], inc=("s_v", 1))
            t_kact = P.tok("s_v")

            for m in range(8):
                E("tensor", lambda h, m=m: h.matmul(
                    ps[4][0:1, 0:512], onesb[:, 0:1], cast[:, m * GT:(m + 1) * GT],
                    start=(m == 0), stop=(m == 7)),
                    waits=[cm_tok[m], bank_free[4] if m == 0 else None],
                    inc=("s_t", 1) if m == 7 else None)
            for m in range(8):
                E("tensor", lambda h, m=m: h.matmul(
                    ps[5][0:1, 0:512], onesb[:, 0:1], u[:, m * GT:(m + 1) * GT],
                    start=(m == 0), stop=(m == 7)),
                    waits=[sq_tok[m], bank_free[5] if m == 0 else None],
                    inc=("s_t", 1) if m == 7 else None)
            t_stats2b = P.tok("s_t")
            E("vector", lambda h: h.tensor_scalar_mul(rM, ps[4][0:1, 0:512], 1.0 / D),
              waits=[t_stats2b], inc=("s_v", 1))
            E("vector", lambda h: h.tensor_mul(rV, rM, rM), inc=("s_v", 1))
            t_mrowb = E("vector", lambda h: h.scalar_tensor_tensor(rV, ps[5][0:1, 0:512], 1.0 / D, rV,
                                                                  ALU.mult, ALU.subtract),
                        inc=("s_v", 1))
            bank_free[4] = t_mrowb
            bank_free[5] = t_mrowb
            t_sqrb = E("scalar", lambda h: h.activation(rR, rV, AF.Sqrt, bias=epsc[0:1, 0:1]),
                       waits=[t_mrowb], inc=("s_s", 1))
            t_rstdb = E("vector", lambda h: h.reciprocal(rR, rR), waits=[t_sqrb], inc=("s_v", 1))
            r_read = {-2: r_read_last[0], -1: r_read_last[1]}
            r_tok = {}
            for m in range(8):
                if m == 2:
                    t_mbcb = E("tensor", lambda h: h.matmul(ps[2][:, 0:512], ones128, rM,
                                                            start=True, stop=True),
                               waits=[t_rstdb, bank_free[2]], inc=("s_t", 1))
                    t_rbcb = E("tensor", lambda h: h.matmul(ps[3][:, 0:512], ones128, rR,
                                                            start=True, stop=True),
                               waits=[bank_free[3]], inc=("s_t", 1))
                    h2_tok = {}
                    for kt in range(8):
                        sl = slice(kt * GT, (kt + 1) * GT)
                        E("vector", lambda h, sl=sl: h.tensor_sub(cast[:, sl], xn[:, sl], ps[2][:, 0:512]),
                          waits=[t_mbcb, t_stats2b, t_rbcb], inc=("s_v", 1))
                        E("vector", lambda h, sl=sl: h.tensor_mul(cast[:, sl], cast[:, sl], ps[3][:, 0:512]),
                          inc=("s_v", 1))
                        h2_tok[kt] = E("vector", lambda h, sl=sl, kt=kt: h.tensor_scalar(
                            cast[:, sl], cast[:, sl], g2c[:, kt:kt + 1], be2c[:, kt:kt + 1], ALU.mult, ALU.add),
                          inc=("s_v", 1))
                    t_h2 = P.tok("s_v")
                    bank_free[2] = t_h2
                    bank_free[3] = t_h2
                slot, rd, t_blk = rwkv_stream(0, m, g)
                bk, t_mm = rwkv_mms(slot, rd, t_blk, lambda kt: xr_r[:, kt * GT:(kt + 1) * GT],
                                    extra=[t_mixdone])
                rsl = slice((m % 2) * 512, (m % 2) * 512 + 512)
                r_tok[m] = E("scalar", lambda h, bk=bk, rsl=rsl: h.activation(
                    r_ring[:, rsl], ps[bk][:, 0:512], AF.Sigmoid),
                    waits=[t_mm, r_read[m - 2]], inc=("s_s", 1))
                bank_free[bk] = r_tok[m]
                slot2, rd2, t_blk2 = rwkv_stream(2, m, g)
                bk2, t_mm2 = rwkv_mms(slot2, rd2, t_blk2, lambda kt: hT[:, kt * GT:(kt + 1) * GT],
                                      extra=[t_kact])
                msl = slice(m * GT, (m + 1) * GT)
                t_p = E("vector", lambda h, rsl=rsl, bk2=bk2: h.tensor_tensor(
                    tmp[:, 0:512], ps[bk2][:, 0:512], r_ring[:, rsl], ALU.mult),
                    waits=[t_mm2, r_tok[m]], inc=("s_v", 1))
                bank_free[bk2] = t_p
                r_read[m] = E("vector", lambda h, msl=msl, x=xg: h.tensor_add(
                    x[:, msl], x[:, msl], tmp[:, 0:512]),
                    waits=[t_p], inc=("s_v", 1))
            r_read_last = [r_read[6], r_read[7]]
            t_rwkv = P.tok("s_v")
            t_acc1 = E("vector", lambda h, x=xg: h.tensor_add(x[:, :], x[:, :], xn[:, :]),
                       waits=[t_x1], inc=("s_v", 1))

            gelu_toks = {}
            for f in range(32):
                slot = f % 4
                ring_dst = w1_ring[:, slot * 1024:(slot + 1) * 1024]
                t_blk = stream(ring_dst.rearrange("p (k c) -> p k c", c=128),
                               w1_d.rearrange("(k p) c -> p k c", p=128)[:, :, f * 128:(f + 1) * 128],
                               f"s_w1{slot}", waits=[w1_last[slot]],
                               scratch=("1", f, wsc_1[f]), g=g, flat=ring_dst,
                               q2="sync" if slot % 2 == 0 else "gpsimd")
                bk = nbig()
                for kt in range(8):
                    def mm_1(h, bk=bk, kt=kt, rd=ring_dst):
                        return h.matmul(ps[bk][:, 0:512], rd[:, kt * 128:(kt + 1) * 128],
                                        cast[:, kt * GT:(kt + 1) * GT],
                                        start=(kt == 0), stop=(kt == 7))
                    E("tensor", mm_1, waits=[t_blk, h2_tok[kt], bank_free[bk], t_att_T],
                      inc=("s_t", 1) if kt == 7 else None)
                t_mm = P.tok("s_t")
                w1_last[slot] = t_mm
                tk = E("scalar", lambda h, f=f, bk=bk: h.activation(
                    y1_r[:, f * GT:(f + 1) * GT], ps[bk][:, 0:512], AF.Gelu, bias=b1c[:, f:f + 1]),
                    waits=[t_mm, t_vec], inc=("s_s", 1))
                bank_free[bk] = tk
                gelu_toks[f] = tk
            t_y1 = P.tok("s_s")
            t_mlp1_T = P.tok("s_t")

            if g + 1 < n_groups:
                t_x_next = emit_xload(g + 1)

            mlp2_state = {"m": 0}

            def mlp2_portion(upto):
                while mlp2_state["m"] < upto:
                    m = mlp2_state["m"]
                    mlp2_state["m"] += 1
                    msl = slice(m * GT, (m + 1) * GT)
                    bk = nbig()
                    for hf in range(2):
                        idx = m * 2 + hf
                        slot = idx % 3
                        ring_dst = w2_ring[:, slot * 2048:(slot + 1) * 2048]
                        t_blk = stream(ring_dst.rearrange("p (k c) -> p k c", c=128),
                                       w2_d.rearrange("(k p) c -> p k c", p=128)[:, hf * 16:(hf + 1) * 16,
                                                                                 m * 128:(m + 1) * 128],
                                       f"s_w2{slot}", waits=[w2_last[slot]],
                                       scratch=("2", idx, wsc_2[idx]), g=g, flat=ring_dst,
                                       q2="gpsimd" if slot == 2 else "sync")
                        for kt in range(16):
                            gk = hf * 16 + kt
                            def mm_2(h, bk=bk, kt=kt, gk=gk, rd=ring_dst):
                                return h.matmul(ps[bk][:, 0:512], rd[:, kt * 128:(kt + 1) * 128],
                                                y1_r[:, gk * GT:(gk + 1) * GT],
                                                start=(gk == 0), stop=(gk == 31))
                            E("tensor", mm_2, waits=[t_blk, gelu_toks[gk], bank_free[bk]],
                              inc=("s_t", 1) if kt == 15 else None)
                        w2_last[slot] = P.tok("s_t")
                    t_mm = P.tok("s_t")
                    bank_free[bk] = E("vector", lambda h, msl=msl, bk=bk, m=m, x=xg: h.scalar_tensor_tensor(
                        x[:, msl], ps[bk][:, 0:512], b2c[:, m:m + 1], x[:, msl], ALU.add, ALU.add),
                        waits=[t_mm, t_acc1], inc=("s_v", 1))

            if g + 1 < n_groups:
                ln1_toks = emit_ln1(g + 1, t_x_next, t_mlp1_T,
                                    mlp2_hook=mlp2_portion)
                mlp2_portion(8)
            else:
                mlp2_portion(8)
            t_mlp = P.tok("s_v")
            mlp2_done = P.tok("s_t")

            t_out = E("scalar", lambda h, x=xg, b=batch, gs=gslice: h.dma_start(
                out=oT.rearrange("b (k p) t -> b p k t", p=128)[b:b + 1, :, :, gs]
                      .rearrange("b p k t -> p (b k) t"),
                in_=x[:].rearrange("p (k t) -> p k t", t=GT)),
                waits=[t_mlp], inc=(f"s_o{g % 2}", 16))
            out_toks.append(t_out)

        E("sync", lambda h: h.nop(), waits=[P.tok("s_o0"), P.tok("s_o1")])

        @block.tensor
        def _(t):
            P.run("tensor", t, sems)

        @block.vector
        def _(v):
            P.run("vector", v, sems)

        @block.scalar
        def _(s):
            P.run("scalar", s, sems)

        @block.sync
        def _(s):
            P.run("sync", s, sems)

        @block.gpsimd
        def _(g):
            P.run("gpsimd", g, sems)

    return nc


def kernel(L0_attn_x, L0_attn_wq, L0_attn_wk, L0_attn_wv, L0_attn_wo, L0_attn_proj,
           L0_attn_w1, L0_attn_b1, L0_attn_w2, L0_attn_b2, L0_attn_g1, L0_attn_beta1,
           L0_attn_g2, L0_attn_beta2, L0_ffn_W_r, L0_ffn_W_k, L0_ffn_W_v,
           L0_ffn_mu_r, L0_ffn_mu_k):
    f32 = np.float32
    x = np.ascontiguousarray(np.asarray(L0_attn_x, dtype=f32))
    n_cores = 8

    asc = lambda a: np.ascontiguousarray(np.asarray(a, dtype=f32))
    wq, wk, wv, wo = asc(L0_attn_wq), asc(L0_attn_wk), asc(L0_attn_wv), asc(L0_attn_wo)
    proj = asc(L0_attn_proj)
    w1, w2 = asc(L0_attn_w1), asc(L0_attn_w2)
    b1, b2 = asc(L0_attn_b1), asc(L0_attn_b2)
    g1, be1, g2, be2 = asc(L0_attn_g1), asc(L0_attn_beta1), asc(L0_attn_g2), asc(L0_attn_beta2)
    Wr, Wk2, Wv2 = asc(L0_ffn_W_r), asc(L0_ffn_W_k), asc(L0_ffn_W_v)
    mur, muk = asc(L0_ffn_mu_r), asc(L0_ffn_mu_k)

    pblk = np.zeros((128, 128), f32)
    pblk[0:64, 0:64] = proj
    pblk[64:128, 64:128] = proj
    cols = lambda v: np.ascontiguousarray(v.reshape(-1, 128).T)
    vecs = np.zeros((128, 64), f32)
    vecs[:, 0:8] = cols(g1)
    vecs[:, 8:16] = cols(be1)
    vecs[:, 16:24] = cols(g2)
    vecs[:, 24:32] = cols(be2)
    vecs[:, 32:40] = cols(mur)
    vecs[:, 40:48] = cols(muk)
    vecs[:, 48:56] = cols(b2)
    b1c = cols(b1)
    maskc = np.ascontiguousarray(np.tile(np.triu(np.ones((128, 128), f32)), (1, 4)))
    idc = np.eye(128, dtype=f32)

    shared = {
        "wqT": np.ascontiguousarray(wq.T), "wkT": np.ascontiguousarray(wk.T),
        "wv": wv, "wo": wo, "pblk": pblk,
        "wrT": np.ascontiguousarray(Wr.T), "wkT2": np.ascontiguousarray(Wk2.T),
        "wvT2": np.ascontiguousarray(Wv2.T),
        "w1": w1, "w2": w2, "vecs": vecs, "b1c": b1c, "maskc": maskc, "idc": idc,
    }
    in_maps = []
    for c in range(n_cores):
        xc = np.ascontiguousarray(np.swapaxes(x[2 * c:2 * c + 2], 1, 2))
        m = dict(shared)
        m["xT"] = xc
        in_maps.append(m)

    nc = build_nc()
    res = run_bass_kernel_spmd(nc, in_maps, core_ids=list(range(n_cores)))
    outs = []
    for c in range(n_cores):
        outs.append(np.swapaxes(res.results[c]["oT"], 1, 2))
    return np.ascontiguousarray(np.concatenate(outs, axis=0))


# version 51
# speedup vs baseline: 1.0193x; 1.0065x over previous
import numpy as np
import ml_dtypes
from contextlib import ExitStack

import concourse.bass as bass
import concourse.mybir as mybir
from concourse.bass_utils import run_bass_kernel_spmd

F32 = mybir.dt.float32
BF16 = mybir.dt.bfloat16
AF = mybir.ActivationFunctionType
ALU = mybir.AluOpType

D = 1024
FF = 4096
NH = 16
DH = 64
SEQ = 2048
GT = 512
NG = 8
NCHUNK = 4
EPS_LN = 1e-5
EPS_ATT = 1e-6


class Prog:
    ENGS = ("tensor", "vector", "scalar", "sync", "gpsimd")

    def __init__(self):
        self.ops = {e: [] for e in self.ENGS}
        self.count = {}
        self.waited = {e: {} for e in self.ENGS}

    def tok(self, sem):
        return (sem, self.count.get(sem, 0))

    def emit(self, eng, fn, waits=(), inc=None):
        ws = []
        for w in waits:
            if w is None:
                continue
            s, v = w
            if v > 0 and self.waited[eng].get(s, 0) < v:
                self.waited[eng][s] = v
                ws.append((s, v))
        tok = None
        if inc is not None:
            s, k = inc
            self.count[s] = self.count.get(s, 0) + k
            tok = (s, self.count[s])
        self.ops[eng].append((ws, fn, inc))
        return tok

    def run(self, eng_name, handle, sems):
        for ws, fn, inc in self.ops[eng_name]:
            for s, v in ws:
                handle.wait_ge(sems[s], v)
            inst = fn(handle)
            if inc is not None:
                s, k = inc
                inst.then_inc(sems[s], k)


def build_nc(n_groups=NG):
    nc = bass.Bass("TRN2", target_bir_lowering=False, detect_race_conditions=False)

    ap = lambda name, shape, dt: nc.dram_tensor(name, shape, dt, kind="ExternalInput").ap()
    xT = ap("xT", [2, D, SEQ], F32)
    wqT = ap("wqT", [D, D], F32)
    wkT = ap("wkT", [D, D], F32)
    wv_d = ap("wv", [D, D], F32)
    wo_d = ap("wo", [D, D], F32)
    pblk = ap("pblk", [128, 128], F32)
    wrT = ap("wrT", [D, D], F32)
    wkT2 = ap("wkT2", [D, D], F32)
    wvT2 = ap("wvT2", [D, D], F32)
    w1_d = ap("w1", [D, FF], F32)
    w2_d = ap("w2", [FF, D], F32)
    vecs = ap("vecs", [128, 64], F32)
    b1c_d = ap("b1c", [128, 32], F32)
    maskc = ap("maskc", [128, 512], F32)
    idc = ap("idc", [128, 128], F32)
    oT = nc.dram_tensor("oT", [2, D, SEQ], F32, kind="ExternalOutput").ap()
    wsc_r = nc.dram_tensor("wsc_r", [24, 128, 1024], BF16, kind="Internal").ap()
    wsc_1 = nc.dram_tensor("wsc_1", [32, 128, 1024], BF16, kind="Internal").ap()
    wsc_2 = nc.dram_tensor("wsc_2", [16, 128, 2048], BF16, kind="Internal").ap()

    P = Prog()
    E = P.emit

    with ExitStack() as es:
        sb = lambda name, shape, dt: es.enter_context(nc.sbuf_tensor(name, shape, dt))
        wqp = sb("wqp", [128, 8 * D], BF16)
        wkp = sb("wkp", [128, 8 * D], BF16)
        wv = sb("wv_sb", [128, 8 * D], BF16)
        xbuf = [sb("xa", [128, 8 * GT], F32), sb("xb2", [128, 8 * GT], F32)]
        xn = sb("xn", [128, 8 * GT], F32)
        cast = sb("cast", [128, 8 * GT], BF16)
        hT = sb("hT", [128, 8 * GT], BF16)
        u = sb("u", [128, 16448], BF16)
        tmp = sb("tmp", [128, GT + 1], F32)
        am = sb("am", [128, 512], BF16)
        wo = sb("wo_sb", [128, 8 * D], BF16)
        den = sb("den", [128, 2], F32)
        rcp = sb("rcp", [128, 2], F32)
        kv = sb("kv", [128, NH * 65], F32)
        kvb = sb("kvb", [128, NH * 65], BF16)
        carry = sb("carry", [128, 8], F32)
        rows = sb("rows", [128, 3 * GT + 128], F32)
        vcol = sb("vcol", [128, 64], F32)
        b1c = sb("b1c_sb", [128, 32], F32)
        maskb = sb("maskb", [128, 512], BF16)
        idb = sb("idb", [128, 128], BF16)
        onesb = sb("onesb", [128, 1], BF16)
        epsc = sb("epsc", [128, 2], F32)
        wr_ring = sb("wr_ring", [128, 4 * 1024], BF16)
        r_ring = sb("r_ring", [128, 2 * 512], BF16)
        w1_ring = sb("w1_ring", [128, 4 * 1024], BF16)
        w2_ring = sb("w2_ring", [128, 3 * 2048], BF16)

        ps = [es.enter_context(nc.psum_tensor(f"ps{i}", [128, 512], F32)) for i in range(8)]
        sems = {}
        for s in (["s_x", "s_o", "s_w", "s_t", "s_v", "s_s"]
                  + [f"s_wr{i}" for i in range(4)]
                  + [f"s_w1{i}" for i in range(4)]
                  + [f"s_w2{i}" for i in range(3)] + ["s_o0", "s_o1"]
                  + [f"s_wr{i}h" for i in range(4)]
                  + [f"s_w1{i}h" for i in range(4)]
                  + [f"s_w2{i}h" for i in range(3)]
                  + ["s_sc", "s_g"]):
            sems[s] = es.enter_context(nc.semaphore(s))
        block = es.enter_context(nc.Block())

        e1 = am
        q_r = u[:, 0:4096]
        k_r = u[:, 4096:8192]
        kt_r = u[:, 8192:12288]
        va_r = u[:, 12288:16448]
        y1_r = u[:, 0:16384]
        xr_r = kt_r
        xk_r = u[:, 12288:16384]
        rM = rows[0:1, 0:GT]
        rV = rows[0:1, GT:2 * GT]
        rR = rows[0:1, 2 * GT:3 * GT]
        ones128 = rows[0:1, 3 * GT:3 * GT + 128]

        g1c = vcol[:, 0:8]
        be1c = vcol[:, 8:16]
        g2c = vcol[:, 16:24]
        be2c = vcol[:, 24:32]
        murc = vcol[:, 32:40]
        mukc = vcol[:, 40:48]
        b2c = vcol[:, 48:56]

        def _ld(dst, src):
            return lambda h, d=dst, s=src: h.dma_start(out=d, in_=s)

        E("sync", _ld(vcol[:, :], vecs), inc=("s_x", 16))
        E("sync", _ld(b1c[:, :], b1c_d), inc=("s_x", 16))
        t_vec = P.tok("s_x")
        E("sync", _ld(xbuf[0][:].rearrange("p (k t) -> p k t", t=GT),
                      xT.rearrange("b (k p) t -> b p k t", p=128)[0:1, :, :, 0:GT]
                        .rearrange("b p k t -> p (b k) t")),
          inc=("s_x", 16))
        t_x0 = P.tok("s_x")

        E("gpsimd", _ld(u[:, 0:8192].rearrange("p (k c) -> p k c", c=1024),
                        wqT.rearrange("(k p) c -> p k c", p=128)), inc=("s_w", 16))
        E("gpsimd", _ld(u[:, 8192:16384].rearrange("p (k c) -> p k c", c=1024),
                        wkT.rearrange("(k p) c -> p k c", p=128)), inc=("s_w", 16))
        E("gpsimd", _ld(am[:, 0:128], pblk), inc=("s_w", 16))
        t_foldw = P.tok("s_w")
        E("gpsimd", _ld(wv[:].rearrange("p (k c) -> p k c", c=1024),
                        wv_d.rearrange("(k p) c -> p k c", p=128)), inc=("s_w", 16))
        t_wv = P.tok("s_w")
        E("gpsimd", _ld(maskb[:, :], maskc), inc=("s_w", 16))
        t_maskb = P.tok("s_w")
        E("gpsimd", _ld(idb[:, :], idc), inc=("s_w", 16))
        t_idb = P.tok("s_w")
        E("gpsimd", _ld(wo[:].rearrange("p (k c) -> p k c", c=1024),
                        wo_d.rearrange("(k p) c -> p k c", p=128)), inc=("s_w", 16))
        t_wstart = P.tok("s_w")

        E("vector", lambda h: h.memset(onesb[:, :], 1.0), inc=("s_v", 1))
        E("vector", lambda h: h.memset(epsc[:, 0:1], EPS_LN), inc=("s_v", 1))
        E("vector", lambda h: h.memset(epsc[:, 1:2], EPS_ATT), inc=("s_v", 1))
        E("vector", lambda h: h.memset(ones128, 1.0), inc=("s_v", 1))
        E("vector", lambda h: h.memset(kv[:, :], 0.0), inc=("s_v", 1))
        E("vector", lambda h: h.memset(kvb[:, :], 0.0), inc=("s_v", 1))
        E("vector", lambda h: h.memset(carry[:, :], 0.0), inc=("s_v", 1))
        t_const = P.tok("s_v")

        fb_free = {2: None, 3: None}
        for dst, stof in ((wqp, 0), (wkp, 8192)):
            for m in range(8):
                for jg in range(2):
                    bank = 2 + jg
                    for j4 in range(4):
                        j = jg * 4 + j4
                        def mm_fold(h, b=bank, j4=j4, j=j, m=m, so=stof):
                            return h.matmul(ps[b][:, j4 * 128:(j4 + 1) * 128],
                                            u[:, so + j * 1024 + m * 128: so + j * 1024 + (m) * 128 + 128],
                                            am[:, 0:128], start=True, stop=True)
                        tk = E("tensor", mm_fold,
                               waits=[t_foldw, t_const, fb_free[bank]],
                               inc=("s_t", 1) if j4 == 3 else None)
                    def ev_fold(h, b=bank, d=dst, m=m, jg=jg):
                        return h.tensor_copy(d[:, m * 1024 + jg * 512: m * 1024 + (jg + 1) * 512],
                                             ps[b][:, 0:512])
                    fb_free[bank] = E("vector", ev_fold, waits=[tk], inc=("s_v", 1))
        t_fold = P.tok("s_v")

        sc_written = {}
        slot_sc = {}

        def stream(dst_ap, src_ap, sem, waits=(), scratch=None, g=0, flat=None, q2=None):
            if scratch is None or g == 0:
                def fn(h, d=dst_ap, s=src_ap):
                    return h.dma_start(out=d, in_=s)
                E("gpsimd", fn, waits=list(waits) + [slot_sc.get(sem)], inc=(sem, 16))
                tok = P.tok(sem)
                if scratch is not None:
                    kind, blk, sc_ap = scratch
                    def wr(h, d=sc_ap, s=flat):
                        return h.dma_start(out=d, in_=s)
                    E("scalar", wr, waits=[tok], inc=("s_sc", 16))
                    sc_written[(kind, blk)] = P.tok("s_sc")
                    slot_sc[sem] = sc_written[(kind, blk)]
                return tok
            else:
                kind, blk, sc_ap = scratch
                q = q2 or ("gpsimd" if kind == "r" else "sync")
                def fn(h, d=flat, s=sc_ap):
                    return h.dma_start(out=d, in_=s)
                E(q, fn, waits=list(waits) + [sc_written[(kind, blk)]], inc=(sem + "h", 16))
                return P.tok(sem + "h")

        bank_free = {i: None for i in range(8)}
        bank_free[0] = t_const
        bank_free[1] = t_const
        bank_free[2] = fb_free[2]
        bank_free[3] = fb_free[3]
        big_free = bank_free
        _bigc = [0]

        def nbig():
            b = (0, 1, 6, 7)[_bigc[0] % 4]
            _bigc[0] += 1
            return b

        _bigc6 = [0]

        def nbig6():
            b = (0, 1, 6, 7, 2, 3)[_bigc6[0] % 6]
            _bigc6[0] += 1
            return b

        out_toks = []
        kv_ready = t_const
        wr_last = [t_fold, t_fold]
        w1_last = [t_fold] * 4
        w2_last = [t_fold] * 3

        wr_last = [t_fold] * 4

        def emit_xload(g):
            batch, c4 = g // 4, g % 4
            gslice = slice(c4 * GT, (c4 + 1) * GT)
            xg = xbuf[g % 2]
            w = [out_toks[g - 2]] if g >= 2 else []
            return E("scalar", _ld(xg[:].rearrange("p (k t) -> p k t", t=GT),
                                 xT.rearrange("b (k p) t -> b p k t", p=128)[batch:batch + 1, :, :, gslice]
                                   .rearrange("b p k t -> p (b k) t")),
                     waits=w, inc=("s_x", 16))

        def emit_ln1(g, t_x, t_mlp1_prev, mlp2_hook=None):
            xg = xbuf[g % 2]
            xb_tok = {}
            for kt in range(8):
                sl = slice(kt * GT, (kt + 1) * GT)
                xb_tok[kt] = E("scalar", lambda h, x=xg, sl=sl: h.activation(
                    cast[:, sl], x[:, sl], AF.Identity),
                    waits=[t_x, t_mlp1_prev], inc=("s_s", 1))
            t_xb = P.tok("s_s")
            if mlp2_hook: mlp2_hook(2)
            for kt in range(8):
                def mm_s1(h, kt=kt):
                    return h.matmul(ps[4][0:1, 0:512], onesb[:, 0:1],
                                    cast[:, kt * GT:(kt + 1) * GT],
                                    start=(kt == 0), stop=(kt == 7))
                E("tensor", mm_s1, waits=[xb_tok[kt], bank_free[4], t_fold],
                  inc=("s_t", 1) if kt == 7 else None)
            t_stats1 = P.tok("s_t")
            t_sq = E("vector", lambda h: h.tensor_mul(cast[:, :], cast[:, :], cast[:, :]),
                     waits=[t_stats1], inc=("s_v", 1))
            if mlp2_hook: mlp2_hook(4)
            for kt in range(8):
                def mm_s2(h, kt=kt):
                    return h.matmul(ps[5][0:1, 0:512], onesb[:, 0:1],
                                    cast[:, kt * GT:(kt + 1) * GT],
                                    start=(kt == 0), stop=(kt == 7))
                E("tensor", mm_s2, waits=[t_sq, bank_free[5]],
                  inc=("s_t", 1) if kt == 7 else None)
            t_stats2 = P.tok("s_t")
            E("vector", lambda h: h.tensor_scalar_mul(rM, ps[4][0:1, 0:512], 1.0 / D),
              waits=[t_stats1], inc=("s_v", 1))
            E("vector", lambda h: h.tensor_mul(rV, rM, rM), inc=("s_v", 1))
            t_mrow = E("vector", lambda h: h.scalar_tensor_tensor(rV, ps[5][0:1, 0:512], 1.0 / D, rV,
                                                                 ALU.mult, ALU.subtract),
                       waits=[t_stats2], inc=("s_v", 1))
            bank_free[4] = t_mrow
            bank_free[5] = t_mrow
            t_sqr = E("scalar", lambda h: h.activation(rR, rV, AF.Sqrt, bias=epsc[0:1, 0:1]),
                      waits=[t_mrow], inc=("s_s", 1))
            t_rstd = E("vector", lambda h: h.reciprocal(rR, rR), waits=[t_sqr], inc=("s_v", 1))
            E("vector", lambda h: h.tensor_mul(rV, rV, rR), waits=[t_rstd], inc=("s_v", 1))
            t_v2 = E("vector", lambda h: h.tensor_mul(rV, rV, rR), inc=("s_v", 1))
            t_s2r = E("scalar", lambda h: h.activation(rV, rV, AF.Sqrt, bias=epsc[0:1, 0:1]),
                      waits=[t_v2], inc=("s_s", 1))
            t_r2 = E("vector", lambda h: h.reciprocal(rV, rV), waits=[t_s2r], inc=("s_v", 1))
            if mlp2_hook: mlp2_hook(6)
            t_mbc = E("tensor", lambda h: h.matmul(ps[2][:, 0:512], ones128, rM,
                                                   start=True, stop=True),
                      waits=[t_r2, bank_free[2]], inc=("s_t", 1))
            t_rbc = E("tensor", lambda h: h.matmul(ps[3][:, 0:512], ones128, rR,
                                                   start=True, stop=True),
                      waits=[bank_free[3]], inc=("s_t", 1))
            t_r2bc = E("tensor", lambda h: h.matmul(ps[4][:, 0:512], ones128, rV,
                                                    start=True, stop=True),
                       waits=[bank_free[4]], inc=("s_t", 1))
            for kt in range(8):
                sl = slice(kt * GT, (kt + 1) * GT)
                E("vector", lambda h, sl=sl, x=xg: h.tensor_sub(xn[:, sl], x[:, sl], ps[2][:, 0:512]),
                  waits=[t_mbc, t_x], inc=("s_v", 1))
            for kt in range(8):
                sl = slice(kt * GT, (kt + 1) * GT)
                E("vector", lambda h, sl=sl: h.tensor_mul(xn[:, sl], xn[:, sl], ps[3][:, 0:512]),
                  waits=[t_rbc], inc=("s_v", 1))
            t_xn = P.tok("s_v")
            if mlp2_hook: mlp2_hook(7)
            h_tok = {}
            for kt in range(8):
                sl = slice(kt * GT, (kt + 1) * GT)
                E("vector", lambda h, sl=sl: h.tensor_mul(hT[:, sl], xn[:, sl], ps[4][:, 0:512]),
                  waits=[t_r2bc, t_xn], inc=("s_v", 1))
                h_tok[kt] = E("vector", lambda h, sl=sl, kt=kt: h.tensor_scalar(
                    hT[:, sl], hT[:, sl], g1c[:, kt:kt + 1], be1c[:, kt:kt + 1], ALU.mult, ALU.add),
                  waits=[t_vec], inc=("s_v", 1))
            t_h = P.tok("s_v")
            bank_free[2] = t_xn
            bank_free[3] = t_xn
            bank_free[4] = t_h
            if mlp2_hook: mlp2_hook(8)
            return {"t_h": t_h, "t_xn": t_xn, "t_stats2": t_stats2, "h_tok": h_tok}

        ln1_toks = emit_ln1(0, t_x0, ("s_t", 0))
        out_toks = []
        kv_ready = t_const
        mlp2_done = ("s_t", 0)
        r_read_last = [("s_v", 0), ("s_v", 0)]
        phi_prev = [None, None]

        for g in range(n_groups):
            batch, c4 = g // 4, g % 4
            xg = xbuf[g % 2]
            gslice = slice(c4 * GT, (c4 + 1) * GT)
            t_h = ln1_toks["t_h"]
            h_tok = ln1_toks["h_tok"]
            t_xn = ln1_toks["t_xn"]
            t_stats2 = ln1_toks["t_stats2"]

            if g == 4:
                tA = P.tok("s_t")
                tS = P.tok("s_s")
                tV = P.tok("s_v")
                E("vector", lambda h: h.memset(kv[:, :], 0.0), waits=[tA, tS, tV], inc=("s_v", 1))
                E("vector", lambda h: h.memset(kvb[:, :], 0.0), inc=("s_v", 1))
                t_reset = P.tok("s_v")
                kv_ready = t_reset
            else:
                t_reset = None

            t_vaok = P.tok("s_t")
            t_vones = E("vector", lambda h: h.memset(
                va_r[:].rearrange("p (c hh o) -> p c hh o", hh=NH, o=65)[:, :, :, 64:65], 1.0),
                waits=[t_vaok, t_h], inc=("s_v", 1))
            e1bufs = (e1[:, 0:512], tmp[:, 0:512])

            def phi_evac(dst, msl, bk, wtok, idx):
                eb = e1bufs[idx % 2]
                w0 = [wtok, phi_prev[idx % 2]]
                if idx == 0:
                    w0 += [P.tok("s_t"), r_read_last[1]]
                t2 = E("scalar", lambda h, bk=bk, eb=eb: h.activation(eb, ps[bk][:, 0:512], AF.Exp),
                       waits=w0, inc=("s_s", 1))
                t1 = E("vector", lambda h, eb=eb: h.tensor_scalar_min(eb, eb, 1.0),
                       waits=[t2], inc=("s_v", 1))
                t3 = E("vector", lambda h, dst=dst, msl=msl, bk=bk, eb=eb: h.scalar_tensor_tensor(
                    dst[:, msl], ps[bk][:, 0:512], 0.0, eb, ALU.max, ALU.add),
                    waits=[t1], inc=("s_v", 1))
                phi_prev[idx % 2] = t3
                return t3

            for qi, (wmat, dstbuf) in enumerate(((wqp, q_r), (wkp, k_r))):
                for m in range(8):
                    bk = nbig6()
                    for kt in range(8):
                        def mm_q(h, bk=bk, kt=kt, m=m, w=wmat):
                            return h.matmul(ps[bk][:, 0:512],
                                            w[:, kt * 1024 + m * 128: kt * 1024 + (m + 1) * 128],
                                            hT[:, kt * GT:(kt + 1) * GT],
                                            start=(kt == 0), stop=(kt == 7))
                        E("tensor", mm_q, waits=[t_h, bank_free[bk], t_fold],
                          inc=("s_t", 1) if kt == 7 else None)
                    t_mm = P.tok("s_t")
                    bank_free[bk] = phi_evac(dstbuf, slice(m * GT, (m + 1) * GT), bk, t_mm,
                                             qi * 8 + m)
            t_qk = P.tok("s_v")


            for tau in range(4):
                for half in range(2):
                    bk = nbig()
                    for kt in range(8):
                        def mm_v(h, bk=bk, kt=kt, tau=tau, half=half):
                            return h.matmul(ps[bk][:, 0:512],
                                            hT[:, kt * GT + tau * 128: kt * GT + tau * 128 + 128],
                                            wv[:, kt * 1024 + half * 512: kt * 1024 + (half + 1) * 512],
                                            start=(kt == 0), stop=(kt == 7))
                        E("tensor", mm_v, waits=[bank_free[bk], t_vones, t_wv],
                          inc=("s_t", 1) if kt == 7 else None)
                    t_mm = P.tok("s_t")
                    def ev_v(h, bk=bk, tau=tau, half=half):
                        dst = va_r[:, tau * 1040 + half * 8 * 65: tau * 1040 + (half * 8 + 8) * 65]                             .rearrange("p (hh c) -> p hh c", c=65)[:, :, 0:64]
                        src = ps[bk][:, 0:512].rearrange("p (hh c) -> p hh c", c=64)
                        return h.tensor_copy(dst, src)
                    bank_free[bk] = E("vector", ev_v, waits=[t_mm], inc=("s_v", 1))

            evt_tok = {}
            for tau in range(4):
                for eg in range(2):
                    bk = nbig()
                    for e4 in range(4):
                        et = eg * 4 + e4
                        def mm_t(h, bk=bk, e4=e4, et=et, tau=tau):
                            return h.matmul(ps[bk][:, e4 * 128:(e4 + 1) * 128],
                                            k_r[:, et * GT + tau * 128: et * GT + tau * 128 + 128],
                                            idb[:, :], start=True, stop=True)
                        E("tensor", mm_t, waits=[t_qk, bank_free[bk], t_idb],
                          inc=("s_t", 1) if e4 == 3 else None)
                    t_mm = P.tok("s_t")
                    def ev_t(h, bk=bk, tau=tau, eg=eg):
                        return h.activation(kt_r[:, tau * 1024 + eg * 512: tau * 1024 + (eg + 1) * 512],
                                            ps[bk][:, 0:512], AF.Identity)
                    evt_tok[(tau, eg)] = E("scalar", ev_t, waits=[t_mm], inc=("s_s", 1))
                    bank_free[bk] = evt_tok[(tau, eg)]
            t_ktok = P.tok("s_s")

            def emit_A(c, hd):
                po = 64 * (hd % 2)
                et = hd // 2
                aslot = hd % 4
                abank = 2 + (hd % 2)
                ksl = slice(et * GT + c * 128, et * GT + c * 128 + 128)
                t_a = E("tensor", lambda h, po=po, ksl=ksl, ab=abank: h.matmul(
                    ps[ab][0:128, 0:128], k_r[po:po + 64, ksl], q_r[po:po + 64, ksl],
                    start=True, stop=True),
                    waits=[evt_tok[(c, hd // 8)], bank_free[abank]], inc=("s_t", 1))
                t_am = E("vector", lambda h, a=aslot, ab=abank: h.tensor_tensor(
                    am[:, a * 128:(a + 1) * 128], ps[ab][0:128, 0:128], maskb[:, 0:128], ALU.mult),
                    waits=[t_a, t_maskb], inc=("s_v", 1))
                bank_free[abank] = t_am
                return t_am

            if g == 4:
                E("vector", lambda h: h.memset(carry[:, :], 0.0), inc=("s_v", 1))

            mix_pend = []

            def queue_mix(kt):
                s0 = kt * GT
                tsb = P.tok("s_t")
                mix_pend.append(lambda tsb=tsb, s0=s0: E(
                    "vector", lambda h, s0=s0: h.tensor_sub(
                        tmp[:, 1:512], xn[:, s0 + 1:s0 + 512], xn[:, s0:s0 + 511]),
                    waits=[tsb, t_xn], inc=("s_v", 1)))
                mix_pend.append(lambda s0=s0, kt=kt: E(
                    "vector", lambda h: h.tensor_sub(
                        tmp[:, 0:1], xn[:, s0:s0 + 1], carry[:, kt:kt + 1]), inc=("s_v", 1)))
                mix_pend.append(lambda s0=s0, kt=kt: E(
                    "vector", lambda h: h.scalar_tensor_tensor(
                        xr_r[:, s0 + 1:s0 + 512], tmp[:, 1:512], murc[:, kt:kt + 1],
                        xn[:, s0:s0 + 511], ALU.mult, ALU.add), inc=("s_v", 1)))
                mix_pend.append(lambda s0=s0, kt=kt: E(
                    "vector", lambda h: h.scalar_tensor_tensor(
                        xr_r[:, s0:s0 + 1], tmp[:, 0:1], murc[:, kt:kt + 1],
                        carry[:, kt:kt + 1], ALU.mult, ALU.add), inc=("s_v", 1)))
                mix_pend.append(lambda s0=s0, kt=kt: E(
                    "vector", lambda h: h.scalar_tensor_tensor(
                        xk_r[:, s0 + 1:s0 + 512], tmp[:, 1:512], mukc[:, kt:kt + 1],
                        xn[:, s0:s0 + 511], ALU.mult, ALU.add), inc=("s_v", 1)))
                mix_pend.append(lambda s0=s0, kt=kt: E(
                    "vector", lambda h: h.scalar_tensor_tensor(
                        xk_r[:, s0:s0 + 1], tmp[:, 0:1], mukc[:, kt:kt + 1],
                        carry[:, kt:kt + 1], ALU.mult, ALU.add), inc=("s_v", 1)))

            def drain_mix(n):
                for _ in range(min(n, len(mix_pend))):
                    mix_pend.pop(0)()

            o_tok = {}
            for c in range(NCHUNK):
                am_toks = {}
                for hh in range(4):
                    am_toks[hh] = emit_A(c, hh)
                for hp in range(NH // 2):
                    nbank = (4, 5, 0, 1)[hp % 4]
                    pair_n = {}
                    for sub in range(2):
                        hd = 2 * hp + sub
                        po = 64 * (hd % 2)
                        et = hd // 2
                        sbank = 6 + (hd % 2)
                        ksl = slice(et * GT + c * 128, et * GT + c * 128 + 128)
                        vsl = slice(c * 1040 + hd * 65, c * 1040 + (hd + 1) * 65)
                        aslot = hd % 4
                        co = sub * 65
                        E("tensor", lambda h, a=aslot, nb=nbank, vsl=vsl, co=co: h.matmul(
                            ps[nb][0:128, co:co + 65], am[:, a * 128:(a + 1) * 128], va_r[:, vsl],
                            start=True, stop=False),
                            waits=[am_toks[hd], bank_free[nbank] if sub == 0 else None, kv_ready])
                        t_n = E("tensor", lambda h, po=po, ksl=ksl, nb=nbank, hd=hd, co=co: h.matmul(
                            ps[nb][0:128, co:co + 65], q_r[po:po + 64, ksl],
                            kvb[po:po + 64, hd * 65:(hd + 1) * 65],
                            start=False, stop=True), inc=("s_t", 1))
                        pair_n[sub] = t_n
                    hd0 = 2 * hp
                    sbank = 6 + (hp % 2)
                    t_s2 = E("tensor", lambda h, c=c, hp=hp, sb_=sbank: h.matmul(
                        ps[sb_][:, 0:130], kt_r[:, c * 1024 + hp * 128: c * 1024 + hp * 128 + 128],
                        va_r[:, c * 1040 + hp * 130: c * 1040 + hp * 130 + 130],
                        start=True, stop=True),
                        waits=[bank_free[sbank]], inc=("s_t", 1))
                    t_kv = E("vector", lambda h, hp=hp, sb_=sbank: h.tensor_add(
                        kv[:, hp * 130:(hp + 1) * 130], kv[:, hp * 130:(hp + 1) * 130],
                        ps[sb_][:, 0:130]),
                        waits=[t_s2], inc=("s_v", 1))
                    bank_free[sbank] = t_kv
                    if hd0 + 4 < NH:
                        am_toks[hd0 + 4] = emit_A(c, hd0 + 4)
                        am_toks[hd0 + 5] = emit_A(c, hd0 + 5)
                    t_d = E("scalar", lambda h, nb=nbank: h.activation(
                        den[:, 0:2], ps[nb][:, 0:130].rearrange("p (hh z) -> p hh z", z=65)[:, :, 64:65]
                            .rearrange("p hh z -> p (hh z)"),
                        AF.Identity, bias=epsc[:, 1:2]),
                        waits=[pair_n[1]], inc=("s_s", 1))
                    t_rc = E("vector", lambda h: h.reciprocal(rcp[:, 0:2], den[:, 0:2]),
                             waits=[t_d], inc=("s_v", 1))
                    t_o = E("scalar", lambda h, nb=nbank, c=c, hd=hd0: h.activation(
                        cast[:, c * 1024 + hd * 64: c * 1024 + (hd + 1) * 64],
                        ps[nb][:, 0:64], AF.Identity, scale=rcp[:, 0:1]),
                        waits=[t_rc], inc=("s_s", 1))
                    t_o = E("scalar", lambda h, nb=nbank, c=c, hd=hd0 + 1: h.activation(
                        cast[:, c * 1024 + hd * 64: c * 1024 + (hd + 1) * 64],
                        ps[nb][:, 65:129], AF.Identity, scale=rcp[:, 1:2]),
                        inc=("s_s", 1))
                    o_tok[(c, hp)] = t_o
                    bank_free[nbank] = t_o
                    drain_mix(2)
                kv_ready = E("scalar", lambda h: h.activation(kvb[:, :], kv[:, :], AF.Identity),
                             waits=[t_kv, P.tok("s_t")], inc=("s_s", 1))
                queue_mix(2 * c)
                queue_mix(2 * c + 1)
            t_att_T = P.tok("s_t")
            t_att_S = P.tok("s_s")

            oT_tok = {}
            for eg in range(8):
                bk = nbig()
                for tau in range(4):
                    def mm_ot(h, bk=bk, tau=tau, eg=eg):
                        return h.matmul(ps[bk][:, tau * 128:(tau + 1) * 128],
                                        cast[:, tau * 1024 + eg * 128: tau * 1024 + eg * 128 + 128],
                                        idb[:, :], start=True, stop=True)
                    E("tensor", mm_ot, waits=[o_tok[(3, eg)], bank_free[bk]],
                      inc=("s_t", 1) if tau == 3 else None)
                t_mm = P.tok("s_t")
                oT_tok[eg] = E("vector", lambda h, bk=bk, eg=eg: h.tensor_copy(
                    hT[:, eg * GT:(eg + 1) * GT], ps[bk][:, 0:512]),
                    waits=[t_mm], inc=("s_v", 1))
                bank_free[bk] = oT_tok[eg]
                drain_mix(2)
            t_oT = P.tok("s_v")
            drain_mix(99)
            t_mixdone = P.tok("s_v")
            t_carry = E("vector", lambda h: h.tensor_copy(
                carry[:, :], xn[:].rearrange("p (k t) -> p k t", t=GT)[:, :, GT - 1:GT]),
                waits=[t_mixdone], inc=("s_v", 1))

            x1_tok = {}
            cm_tok = {}
            sq_tok = {}
            for m in range(8):
                bk = nbig()
                for kt in range(8):
                    def mm_wo(h, bk=bk, kt=kt, m=m):
                        return h.matmul(ps[bk][:, 0:512],
                                        wo[:, kt * 1024 + m * 128: kt * 1024 + (m + 1) * 128],
                                        hT[:, kt * GT:(kt + 1) * GT],
                                        start=(kt == 0), stop=(kt == 7))
                    E("tensor", mm_wo, waits=[oT_tok[kt], bank_free[bk], t_wstart],
                      inc=("s_t", 1) if kt == 7 else None)
                t_mm = P.tok("s_t")
                msl = slice(m * GT, (m + 1) * GT)
                x1_tok[m] = E("vector", lambda h, msl=msl, bk=bk: h.tensor_add(
                    xn[:, msl], xn[:, msl], ps[bk][:, 0:512]),
                    waits=[t_mm, t_carry], inc=("s_v", 1))
                bank_free[bk] = x1_tok[m]
                cm_tok[m] = E("scalar", lambda h, msl=msl: h.activation(
                    cast[:, msl], xn[:, msl], AF.Identity),
                    waits=[x1_tok[m], P.tok("s_t") if m == 0 else None], inc=("s_s", 1))
                sq_tok[m] = E("vector", lambda h, msl=msl: h.tensor_mul(
                    u[:, msl], cast[:, msl], cast[:, msl]),
                    waits=[cm_tok[m]], inc=("s_v", 1))
            t_x1 = P.tok("s_v")
            def rwkv_stream(mat_i, m, g):
                idx = mat_i * 8 + m
                slot = rwkv_stream.c % 4
                rwkv_stream.c += 1
                wsrc = (wrT, wkT2, wvT2)[mat_i]
                ring_dst = wr_ring[:, slot * 1024:(slot + 1) * 1024]
                t_blk = stream(ring_dst.rearrange("p (k c) -> p k c", c=128),
                               wsrc.rearrange("(k p) c -> p k c", p=128)[:, :, m * 128:(m + 1) * 128],
                               f"s_wr{slot}",
                               waits=[wr_last[slot]],
                               scratch=("r", idx, wsc_r[idx]), g=g, flat=ring_dst)
                return slot, ring_dst, t_blk
            rwkv_stream.c = 0

            def rwkv_mms(slot, ring_dst, t_blk, rhs_fn, extra=()):
                bk = nbig()
                for kt in range(8):
                    def mm_rwkv(h, bk=bk, kt=kt, rd=ring_dst, rf=rhs_fn):
                        return h.matmul(ps[bk][:, 0:512], rd[:, kt * 128:(kt + 1) * 128],
                                        rf(kt), start=(kt == 0), stop=(kt == 7))
                    E("tensor", mm_rwkv, waits=[t_blk, bank_free[bk]] + list(extra),
                      inc=("s_t", 1) if kt == 7 else None)
                t_mm = P.tok("s_t")
                wr_last[slot] = t_mm
                return bk, t_mm

            for m in range(8):
                slot, rd, t_blk = rwkv_stream(1, m, g)
                bk, t_mm = rwkv_mms(slot, rd, t_blk, lambda kt: xk_r[:, kt * GT:(kt + 1) * GT],
                                    extra=[t_mixdone])
                msl = slice(m * GT, (m + 1) * GT)
                t_rl = E("scalar", lambda h, bk=bk: h.activation(
                    tmp[:, 0:512], ps[bk][:, 0:512], AF.Relu),
                    waits=[t_mm, P.tok("s_v")], inc=("s_s", 1))
                bank_free[bk] = E("vector", lambda h, msl=msl: h.tensor_mul(
                    hT[:, msl], tmp[:, 0:512], tmp[:, 0:512]),
                    waits=[t_rl], inc=("s_v", 1))
            t_kact = P.tok("s_v")

            for m in range(8):
                E("tensor", lambda h, m=m: h.matmul(
                    ps[4][0:1, 0:512], onesb[:, 0:1], cast[:, m * GT:(m + 1) * GT],
                    start=(m == 0), stop=(m == 7)),
                    waits=[cm_tok[m], bank_free[4] if m == 0 else None],
                    inc=("s_t", 1) if m == 7 else None)
            for m in range(8):
                E("tensor", lambda h, m=m: h.matmul(
                    ps[5][0:1, 0:512], onesb[:, 0:1], u[:, m * GT:(m + 1) * GT],
                    start=(m == 0), stop=(m == 7)),
                    waits=[sq_tok[m], bank_free[5] if m == 0 else None],
                    inc=("s_t", 1) if m == 7 else None)
            t_stats2b = P.tok("s_t")
            E("vector", lambda h: h.tensor_scalar_mul(rM, ps[4][0:1, 0:512], 1.0 / D),
              waits=[t_stats2b], inc=("s_v", 1))
            E("vector", lambda h: h.tensor_mul(rV, rM, rM), inc=("s_v", 1))
            t_mrowb = E("vector", lambda h: h.scalar_tensor_tensor(rV, ps[5][0:1, 0:512], 1.0 / D, rV,
                                                                  ALU.mult, ALU.subtract),
                        inc=("s_v", 1))
            bank_free[4] = t_mrowb
            bank_free[5] = t_mrowb
            t_sqrb = E("scalar", lambda h: h.activation(rR, rV, AF.Sqrt, bias=epsc[0:1, 0:1]),
                       waits=[t_mrowb], inc=("s_s", 1))
            t_rstdb = E("vector", lambda h: h.reciprocal(rR, rR), waits=[t_sqrb], inc=("s_v", 1))
            r_read = {-2: r_read_last[0], -1: r_read_last[1]}
            r_tok = {}
            for m in range(8):
                if m == 2:
                    t_mbcb = E("tensor", lambda h: h.matmul(ps[2][:, 0:512], ones128, rM,
                                                            start=True, stop=True),
                               waits=[t_rstdb, bank_free[2]], inc=("s_t", 1))
                    t_rbcb = E("tensor", lambda h: h.matmul(ps[3][:, 0:512], ones128, rR,
                                                            start=True, stop=True),
                               waits=[bank_free[3]], inc=("s_t", 1))
                    h2_tok = {}
                    for kt in range(8):
                        sl = slice(kt * GT, (kt + 1) * GT)
                        E("vector", lambda h, sl=sl: h.tensor_sub(cast[:, sl], xn[:, sl], ps[2][:, 0:512]),
                          waits=[t_mbcb, t_stats2b, t_rbcb], inc=("s_v", 1))
                        E("vector", lambda h, sl=sl: h.tensor_mul(cast[:, sl], cast[:, sl], ps[3][:, 0:512]),
                          inc=("s_v", 1))
                        h2_tok[kt] = E("vector", lambda h, sl=sl, kt=kt: h.tensor_scalar(
                            cast[:, sl], cast[:, sl], g2c[:, kt:kt + 1], be2c[:, kt:kt + 1], ALU.mult, ALU.add),
                          inc=("s_v", 1))
                    t_h2 = P.tok("s_v")
                    bank_free[2] = t_h2
                    bank_free[3] = t_h2
                slot, rd, t_blk = rwkv_stream(0, m, g)
                bk, t_mm = rwkv_mms(slot, rd, t_blk, lambda kt: xr_r[:, kt * GT:(kt + 1) * GT],
                                    extra=[t_mixdone])
                rsl = slice((m % 2) * 512, (m % 2) * 512 + 512)
                r_tok[m] = E("scalar", lambda h, bk=bk, rsl=rsl: h.activation(
                    r_ring[:, rsl], ps[bk][:, 0:512], AF.Sigmoid),
                    waits=[t_mm, r_read[m - 2]], inc=("s_s", 1))
                bank_free[bk] = r_tok[m]
                slot2, rd2, t_blk2 = rwkv_stream(2, m, g)
                bk2, t_mm2 = rwkv_mms(slot2, rd2, t_blk2, lambda kt: hT[:, kt * GT:(kt + 1) * GT],
                                      extra=[t_kact])
                msl = slice(m * GT, (m + 1) * GT)
                t_p = E("vector", lambda h, rsl=rsl, bk2=bk2: h.tensor_tensor(
                    tmp[:, 0:512], ps[bk2][:, 0:512], r_ring[:, rsl], ALU.mult),
                    waits=[t_mm2, r_tok[m]], inc=("s_v", 1))
                bank_free[bk2] = t_p
                r_read[m] = E("vector", lambda h, msl=msl, x=xg: h.tensor_add(
                    x[:, msl], x[:, msl], tmp[:, 0:512]),
                    waits=[t_p], inc=("s_v", 1))
            r_read_last = [r_read[6], r_read[7]]
            t_rwkv = P.tok("s_v")
            t_acc1 = E("vector", lambda h, x=xg: h.tensor_add(x[:, :], x[:, :], xn[:, :]),
                       waits=[t_x1], inc=("s_v", 1))

            gelu_toks = {}
            for f in range(32):
                slot = f % 4
                ring_dst = w1_ring[:, slot * 1024:(slot + 1) * 1024]
                t_blk = stream(ring_dst.rearrange("p (k c) -> p k c", c=128),
                               w1_d.rearrange("(k p) c -> p k c", p=128)[:, :, f * 128:(f + 1) * 128],
                               f"s_w1{slot}", waits=[w1_last[slot]],
                               scratch=("1", f, wsc_1[f]), g=g, flat=ring_dst,
                               q2="sync" if slot % 2 == 0 else "gpsimd")
                bk = nbig()
                for kt in range(8):
                    def mm_1(h, bk=bk, kt=kt, rd=ring_dst):
                        return h.matmul(ps[bk][:, 0:512], rd[:, kt * 128:(kt + 1) * 128],
                                        cast[:, kt * GT:(kt + 1) * GT],
                                        start=(kt == 0), stop=(kt == 7))
                    E("tensor", mm_1, waits=[t_blk, h2_tok[kt], bank_free[bk], t_att_T],
                      inc=("s_t", 1) if kt == 7 else None)
                t_mm = P.tok("s_t")
                w1_last[slot] = t_mm
                tk = E("scalar", lambda h, f=f, bk=bk: h.activation(
                    y1_r[:, f * GT:(f + 1) * GT], ps[bk][:, 0:512], AF.Gelu, bias=b1c[:, f:f + 1]),
                    waits=[t_mm, t_vec], inc=("s_s", 1))
                bank_free[bk] = tk
                gelu_toks[f] = tk
            t_y1 = P.tok("s_s")
            t_mlp1_T = P.tok("s_t")

            if g + 1 < n_groups:
                t_x_next = emit_xload(g + 1)

            mlp2_state = {"m": 0}

            def mlp2_portion(upto):
                while mlp2_state["m"] < upto:
                    m = mlp2_state["m"]
                    mlp2_state["m"] += 1
                    msl = slice(m * GT, (m + 1) * GT)
                    bk = nbig()
                    for hf in range(2):
                        idx = m * 2 + hf
                        slot = idx % 3
                        ring_dst = w2_ring[:, slot * 2048:(slot + 1) * 2048]
                        t_blk = stream(ring_dst.rearrange("p (k c) -> p k c", c=128),
                                       w2_d.rearrange("(k p) c -> p k c", p=128)[:, hf * 16:(hf + 1) * 16,
                                                                                 m * 128:(m + 1) * 128],
                                       f"s_w2{slot}", waits=[w2_last[slot]],
                                       scratch=("2", idx, wsc_2[idx]), g=g, flat=ring_dst,
                                       q2="gpsimd" if slot == 2 else "sync")
                        for kt in range(16):
                            gk = hf * 16 + kt
                            def mm_2(h, bk=bk, kt=kt, gk=gk, rd=ring_dst):
                                return h.matmul(ps[bk][:, 0:512], rd[:, kt * 128:(kt + 1) * 128],
                                                y1_r[:, gk * GT:(gk + 1) * GT],
                                                start=(gk == 0), stop=(gk == 31))
                            E("tensor", mm_2, waits=[t_blk, gelu_toks[gk], bank_free[bk]],
                              inc=("s_t", 1) if kt == 15 else None)
                        w2_last[slot] = P.tok("s_t")
                    t_mm = P.tok("s_t")
                    bank_free[bk] = E("vector", lambda h, msl=msl, bk=bk, m=m, x=xg: h.scalar_tensor_tensor(
                        x[:, msl], ps[bk][:, 0:512], b2c[:, m:m + 1], x[:, msl], ALU.add, ALU.add),
                        waits=[t_mm, t_acc1], inc=("s_v", 1))

            if g + 1 < n_groups:
                ln1_toks = emit_ln1(g + 1, t_x_next, t_mlp1_T,
                                    mlp2_hook=mlp2_portion)
                mlp2_portion(8)
            else:
                mlp2_portion(8)
            t_mlp = P.tok("s_v")
            mlp2_done = P.tok("s_t")

            t_out = E("scalar", lambda h, x=xg, b=batch, gs=gslice: h.dma_start(
                out=oT.rearrange("b (k p) t -> b p k t", p=128)[b:b + 1, :, :, gs]
                      .rearrange("b p k t -> p (b k) t"),
                in_=x[:].rearrange("p (k t) -> p k t", t=GT)),
                waits=[t_mlp], inc=(f"s_o{g % 2}", 16))
            out_toks.append(t_out)

        E("sync", lambda h: h.nop(), waits=[P.tok("s_o0"), P.tok("s_o1")])

        @block.tensor
        def _(t):
            P.run("tensor", t, sems)

        @block.vector
        def _(v):
            P.run("vector", v, sems)

        @block.scalar
        def _(s):
            P.run("scalar", s, sems)

        @block.sync
        def _(s):
            P.run("sync", s, sems)

        @block.gpsimd
        def _(g):
            P.run("gpsimd", g, sems)

    return nc


def kernel(L0_attn_x, L0_attn_wq, L0_attn_wk, L0_attn_wv, L0_attn_wo, L0_attn_proj,
           L0_attn_w1, L0_attn_b1, L0_attn_w2, L0_attn_b2, L0_attn_g1, L0_attn_beta1,
           L0_attn_g2, L0_attn_beta2, L0_ffn_W_r, L0_ffn_W_k, L0_ffn_W_v,
           L0_ffn_mu_r, L0_ffn_mu_k):
    f32 = np.float32
    x = np.ascontiguousarray(np.asarray(L0_attn_x, dtype=f32))
    n_cores = 8

    asc = lambda a: np.ascontiguousarray(np.asarray(a, dtype=f32))
    wq, wk, wv, wo = asc(L0_attn_wq), asc(L0_attn_wk), asc(L0_attn_wv), asc(L0_attn_wo)
    proj = asc(L0_attn_proj)
    w1, w2 = asc(L0_attn_w1), asc(L0_attn_w2)
    b1, b2 = asc(L0_attn_b1), asc(L0_attn_b2)
    g1, be1, g2, be2 = asc(L0_attn_g1), asc(L0_attn_beta1), asc(L0_attn_g2), asc(L0_attn_beta2)
    Wr, Wk2, Wv2 = asc(L0_ffn_W_r), asc(L0_ffn_W_k), asc(L0_ffn_W_v)
    mur, muk = asc(L0_ffn_mu_r), asc(L0_ffn_mu_k)

    pblk = np.zeros((128, 128), f32)
    pblk[0:64, 0:64] = proj
    pblk[64:128, 64:128] = proj
    cols = lambda v: np.ascontiguousarray(v.reshape(-1, 128).T)
    vecs = np.zeros((128, 64), f32)
    vecs[:, 0:8] = cols(g1)
    vecs[:, 8:16] = cols(be1)
    vecs[:, 16:24] = cols(g2)
    vecs[:, 24:32] = cols(be2)
    vecs[:, 32:40] = cols(mur)
    vecs[:, 40:48] = cols(muk)
    vecs[:, 48:56] = cols(b2)
    b1c = cols(b1)
    maskc = np.ascontiguousarray(np.tile(np.triu(np.ones((128, 128), f32)), (1, 4)))
    idc = np.eye(128, dtype=f32)

    shared = {
        "wqT": np.ascontiguousarray(wq.T), "wkT": np.ascontiguousarray(wk.T),
        "wv": wv, "wo": wo, "pblk": pblk,
        "wrT": np.ascontiguousarray(Wr.T), "wkT2": np.ascontiguousarray(Wk2.T),
        "wvT2": np.ascontiguousarray(Wv2.T),
        "w1": w1, "w2": w2, "vecs": vecs, "b1c": b1c, "maskc": maskc, "idc": idc,
    }
    in_maps = []
    for c in range(n_cores):
        xc = np.ascontiguousarray(np.swapaxes(x[2 * c:2 * c + 2], 1, 2))
        m = dict(shared)
        m["xT"] = xc
        in_maps.append(m)

    nc = build_nc()
    res = run_bass_kernel_spmd(nc, in_maps, core_ids=list(range(n_cores)))
    outs = []
    for c in range(n_cores):
        outs.append(np.swapaxes(res.results[c]["oT"], 1, 2))
    return np.ascontiguousarray(np.concatenate(outs, axis=0))
